# version 04
# baseline (speedup 1.0000x reference)
import contextlib
import numpy as np
import ml_dtypes
import concourse.bass as bass
import concourse.mybir as mybir
from concourse.bass_utils import run_bass_kernel_spmd

F32 = mybir.dt.float32
BF16 = mybir.dt.bfloat16
AF = mybir.ActivationFunctionType
ALU = mybir.AluOpType
AX = mybir.AxisListType
BF = ml_dtypes.bfloat16

NCORES = 8
TLOC = 2048
TF = 2560
RB = 512
NCH = 8
CS = 256
HID = 1024
INTER = 2048
NH, HP = 32, 64
SN = 128
CONV_DIM = 2304
PROJ = 4384
PROJ_PAD = 4480
BIG = 30000.0

TENSORS = {
    "x_fm":     ((HID, TF), np.float32, F32, "in"),
    "w_in_t":   ((HID, PROJ_PAD), np.float32, F32, "in"),
    "b_in_sb":  ((128, 35), np.float32, F32, "in"),
    "convw_sb": ((128, 72), np.float32, F32, "in"),
    "dtb_sb":   ((32, 1), np.float32, F32, "in"),
    "alog_sb":  ((32, 1), np.float32, F32, "in"),
    "dexp_sb":  ((128, 16), np.float32, F32, "in"),
    "normw_sb": ((128, 16), np.float32, F32, "in"),
    "bout_sb":  ((128, 8), np.float32, F32, "in"),
    "w_out_t":  ((INTER, HID), np.float32, F32, "in"),
    "ident_b":  ((128, 128), BF, BF16, "in"),
    "ident_f":  ((128, 128), np.float32, F32, "in"),
    "triu_b":   ((256, 256), BF, BF16, "in"),
    "ctx_sb":   ((128, 1), np.float32, F32, "in"),
    "csel_sb":  ((128, 8), np.float32, F32, "in"),
    "wg0_t":    ((HID, 16), np.float32, F32, "in"),
    "w1e_t":    ((16, HID, 512), np.float32, F32, "in"),
    "w3e_t":    ((16, HID, 512), np.float32, F32, "in"),
    "w2e_t":    ((16, 512, HID), np.float32, F32, "in"),
    "w1s_t":    ((HID, 1024), np.float32, F32, "in"),
    "w3s_t":    ((HID, 1024), np.float32, F32, "in"),
    "w2s_t":    ((1024, HID), np.float32, F32, "in"),
    "wq_t":     ((HID, HID), np.float32, F32, "in"),
    "wk_t":     ((HID, HID), np.float32, F32, "in"),
    "wv_t":     ((HID, HID), np.float32, F32, "in"),
    "wp_t":     ((HID, HID), np.float32, F32, "in"),
    "bnp_sb":   ((128, 48), np.float32, F32, "in"),
    "bp_sb":    ((128, 8), np.float32, F32, "in"),
    "wg1_t":    ((HID, 8), np.float32, F32, "in"),
    "l1w1_t":   ((8, HID, 2048), np.float32, F32, "in"),
    "l1w2_t":   ((8, HID, 2048), np.float32, F32, "in"),
    "l1w3_t":   ((8, 2048, HID), np.float32, F32, "in"),
    "xn_s":     ((HID, TF), BF, BF16, "scratch"),
    "xn32_s":   ((HID, TF), np.float32, F32, "scratch"),
    "xn132_s":  ((HID, TLOC), np.float32, F32, "scratch"),
    "gate_s":   ((INTER, TLOC), BF, BF16, "scratch"),
    "hbc_s":    ((CONV_DIM, TF), BF, BF16, "scratch"),
    "dt_s":     ((32, TLOC), np.float32, F32, "scratch"),
    "acum_s":   ((32, TLOC), np.float32, F32, "scratch"),
    "ypart_s":  ((INTER, TLOC), BF, BF16, "scratch"),
    "cfm_s":    ((SN, TLOC), BF, BF16, "scratch"),
    "sst_s":    ((128, NCH * INTER), np.float32, F32, "scratch"),
    "chd_s":    ((32, 8), np.float32, F32, "scratch"),
    "chtot_s":  ((32, 1), np.float32, F32, "scratch"),
    "ec_s":     ((32, 8), np.float32, F32, "scratch"),
    "sdo_s":    ((64, 256), BF, BF16, "scratch"),
    "comb_s":   ((18, TLOC), BF, BF16, "scratch"),
    "hid_s":    ((9216, TLOC), BF, BF16, "scratch"),
    "w2b_s":    ((9216, HID), BF, BF16, "scratch"),
    "comb1_s":  ((8, TLOC), BF, BF16, "scratch"),
    "hid1_s":   ((16384, TLOC), BF, BF16, "scratch"),
    "w3b1_s":   ((16384, HID), BF, BF16, "scratch"),
    "st_loc":   ((129, INTER), np.float32, F32, "scratch"),
    "st_all":   ((129 * NCORES, INTER), np.float32, F32, "scratch"),
    "sstg_s":   ((128, NCH * INTER), BF, BF16, "scratch"),
    "attn0_s":  ((HID, TLOC), np.float32, F32, "scratch"),
    "ffn0_s":   ((HID, TLOC), np.float32, F32, "scratch"),
    "x1_s":     ((HID, TLOC), np.float32, F32, "scratch"),
    "xn1_s":    ((HID, TLOC), BF, BF16, "scratch"),
    "hq_s":     ((HID, TLOC), BF, BF16, "scratch"),
    "hk_s":     ((HID, TLOC), BF, BF16, "scratch"),
    "hv_s":     ((HID, TLOC), BF, BF16, "scratch"),
    "bn_loc":   ((128, 48), np.float32, F32, "scratch"),
    "bn_all":   ((128 * NCORES, 48), np.float32, F32, "scratch"),
    "qk_s":     ((HID, TLOC), BF, BF16, "scratch"),
    "vv_loc":   ((128, 512), np.float32, F32, "scratch"),
    "vv_all":   ((128 * NCORES, 512), np.float32, F32, "scratch"),
    "attn1_s":  ((HID, TLOC), np.float32, F32, "scratch"),
    "ffn1_s":   ((HID, TLOC), np.float32, F32, "scratch"),
    "y_out":    ((HID, TLOC), np.float32, F32, "out"),
}

ALL_SEMS = ["qs", "qp", "pe", "dve", "act", "pool", "cc1", "cc2", "cc3"]


class Env:
    def __init__(self, nc):
        self.nc = nc
        self.t = {}
        self.sb = {}
        self._sb_guard = {}
        self.sem = {}
        self.cnt = {}
        self._stack = contextlib.ExitStack()
        self._order = []
        for s in ALL_SEMS:
            self.sem[s] = self._stack.enter_context(nc.semaphore(s))
            self.cnt[s] = 0

    def sget(self, name):
        if name not in self.sem:
            self.sem[name] = self._stack.enter_context(self.nc.semaphore(name))
            self.cnt[name] = 0
        return self.sem[name]

    def inc(self, name, instr, n=1):
        self.sget(name)
        instr.then_inc(self.sem[name], n)
        self.cnt[name] += n
        return self.cnt[name]

    def waits(self, e, B):
        for s, v in B.items():
            if v > 0:
                e.wait_ge(self.sem[s], v)

    def w(self, e, name, v):
        e.wait_ge(self.sget(name), v)

    def alloc(self, name, shape, dt):
        g = self.nc.sbuf_tensor(name, list(shape), dt)
        self._sb_guard[name] = g
        self._order.append(name)
        self.sb[name] = g.__enter__()
        return self.sb[name]

    def mark(self):
        return len(self._order)

    def release(self, mark):
        while len(self._order) > mark:
            name = self._order.pop()
            self._sb_guard.pop(name).__exit__(None, None, None)
            del self.sb[name]

    def close(self):
        self.release(0)
        self._stack.close()


class Sched:
    ENG_SEM = {"sync": "qs", "vector": "dve", "scalar": "act",
               "tensor": "pe", "gpsimd": "pool"}

    def __init__(self, env, B):
        self.env, self.B = env, B
        self.ops = {e: [] for e in ("sync", "vector", "scalar", "tensor", "gpsimd")}
        self.keyinfo = {}

    def add(self, eng, key, deps, emit, dma_sem=None, drain=True):
        env = self.env
        if dma_sem is not None:
            env.sget(dma_sem)
            env.cnt[dma_sem] += 16
            info = (dma_sem, env.cnt[dma_sem], 16)
        else:
            sem = self.ENG_SEM[eng]
            env.cnt[sem] += 1
            info = (sem, env.cnt[sem], 1)
        if key is not None:
            assert key not in self.keyinfo, key
            self.keyinfo[key] = info
        self.ops[eng].append((key, tuple(deps), emit, info, drain))

    def emit(self, blk):
        env = self.env
        for eng_name in ("sync", "vector", "scalar", "tensor", "gpsimd"):
            ops = self.ops[eng_name]
            if not ops:
                continue
            def body(e, ops=ops, eng_name=eng_name):
                env.waits(e, self.B)
                hi = {}
                for key, deps, emit, (sem, cnt, n), drn in ops:
                    for d in deps:
                        dsem, dcnt, _ = self.keyinfo[d]
                        if hi.get(dsem, self.B.get(dsem, 0)) < dcnt:
                            e.wait_ge(env.sget(dsem), dcnt)
                            hi[dsem] = dcnt
                    instr = emit(e)
                    if eng_name == "tensor" and drn:
                        instr = e.drain()
                    instr.then_inc(env.sget(sem), n)
            getattr(blk, eng_name)(body)


def _col128(v, ncols):
    n = v.shape[0]
    out = np.zeros((ncols * 128,), v.dtype)
    out[:n] = v
    return np.ascontiguousarray(out.reshape(ncols, 128).T)


def host_prep(inp):
    g = {}
    w_in = np.asarray(inp["L0_attn_W_in"])
    w_in_pad = np.zeros((PROJ_PAD, HID), np.float32)
    w_in_pad[:PROJ] = w_in
    g["w_in_t"] = np.ascontiguousarray(w_in_pad.T)
    g["b_in_sb"] = _col128(np.asarray(inp["L0_attn_b_in"]), 35)
    cw = np.asarray(inp["L0_attn_conv_w"])
    g["convw_sb"] = np.ascontiguousarray(cw.reshape(18, 128, 4).transpose(1, 0, 2).reshape(128, 72))
    g["dtb_sb"] = np.asarray(inp["L0_attn_dt_bias"]).reshape(32, 1).astype(np.float32)
    g["alog_sb"] = np.asarray(inp["L0_attn_A_log"]).reshape(32, 1).astype(np.float32)
    g["dexp_sb"] = _col128(np.repeat(np.asarray(inp["L0_attn_D"]), HP), 16)
    g["normw_sb"] = _col128(np.asarray(inp["L0_attn_norm_w"]), 16)
    g["bout_sb"] = _col128(np.asarray(inp["L0_attn_b_out"]), 8)
    g["w_out_t"] = np.ascontiguousarray(np.asarray(inp["L0_attn_W_out"]).T)
    g["ident_b"] = np.eye(128, dtype=BF)
    g["ident_f"] = np.eye(128, dtype=np.float32)
    g["triu_b"] = np.triu(np.ones((256, 256), np.float32)).astype(BF)
    g["wg0_t"] = np.ascontiguousarray(np.asarray(inp["L0_ffn_w_gate"]).T)
    g["w1e_t"] = np.ascontiguousarray(np.asarray(inp["L0_ffn_w1_e"]).transpose(0, 2, 1))
    g["w3e_t"] = np.ascontiguousarray(np.asarray(inp["L0_ffn_w3_e"]).transpose(0, 2, 1))
    g["w2e_t"] = np.ascontiguousarray(np.asarray(inp["L0_ffn_w2_e"]).transpose(0, 2, 1))
    g["w1s_t"] = np.ascontiguousarray(np.asarray(inp["L0_ffn_w1_s"]).T)
    g["w3s_t"] = np.ascontiguousarray(np.asarray(inp["L0_ffn_w3_s"]).T)
    g["w2s_t"] = np.ascontiguousarray(np.asarray(inp["L0_ffn_w2_s"]).T)
    g["wq_t"] = np.ascontiguousarray(np.asarray(inp["L1_attn_Wq"]).T)
    g["wk_t"] = np.ascontiguousarray(np.asarray(inp["L1_attn_Wk"]).T)
    g["wv_t"] = np.ascontiguousarray(np.asarray(inp["L1_attn_Wv"]).T)
    g["wp_t"] = np.ascontiguousarray(np.asarray(inp["L1_attn_Wp"]).T)
    g["bnp_sb"] = np.concatenate(
        [_col128(np.asarray(inp[f"L1_attn_{k}"]), 8) for k in ("gq", "bq", "gk", "bk", "gv", "bv")], axis=1)
    g["bp_sb"] = _col128(np.asarray(inp["L1_attn_bp"]), 8)
    g["wg1_t"] = np.ascontiguousarray(np.asarray(inp["L1_ffn_Wg"]).T)
    g["l1w1_t"] = np.ascontiguousarray(np.asarray(inp["L1_ffn_W1"]).transpose(0, 2, 1))
    g["l1w2_t"] = np.ascontiguousarray(np.asarray(inp["L1_ffn_W2"]).transpose(0, 2, 1))
    g["l1w3_t"] = np.ascontiguousarray(np.asarray(inp["L1_ffn_W3"]).transpose(0, 2, 1))

    x = np.asarray(inp["L0_attn_x"])[0]
    per_core = []
    for c in range(NCORES):
        d = dict(g)
        xc = np.zeros((TF, HID), np.float32)
        if c > 0:
            xc[RB - 3:RB] = x[c * TLOC - 3:c * TLOC]
        xc[RB:] = x[c * TLOC:(c + 1) * TLOC]
        d["x_fm"] = np.ascontiguousarray(xc.T)
        d["ctx_sb"] = np.full((128, 1), 0.0 if c == 0 else 1.0, np.float32)
        cs = np.zeros((128, 8), np.float32)
        if c > 0:
            cs[:, c - 1] = 1.0
        d["csel_sb"] = cs
        per_core.append(d)
    return per_core


def st_ln1(env):
    nc = env.nc
    B = dict(env.cnt)
    T = env.t
    mk = env.mark()
    xt = env.alloc("ln_x", (128, 8, 512), F32)
    x2 = env.alloc("ln_x2", (128, 8, 512), BF16)
    xb = env.alloc("ln_xb", (128, 8, 512), BF16)
    onesb = env.alloc("ln_onesb", (128, 1), BF16)
    onesf = env.alloc("ln_onesf", (1, 128), F32)
    st0 = env.alloc("ln_st0", (1, 512), F32)
    ms = env.alloc("ln_ms", (1, 2, 512), F32)
    xo = env.alloc("ln_out", (128, 2, 8, 512), BF16)
    xo32 = env.alloc("ln_out32", (128, 2, 8, 512), F32)
    x_r = T["x_fm"].ap().rearrange("(c p) t -> p c t", p=128)
    xn_r = T["xn_s"].ap().rearrange("(c p) t -> p c t", p=128)
    xn32_r = T["xn32_s"].ap().rearrange("(c p) t -> p c t", p=128)
    NTT = 5
    D, A, P, QS, QP = B["dve"] + 2, B["act"], B["pe"], B["qs"], B["qp"]
    with (
        nc.psum_tensor("ps_st", [1, 2, 512], F32) as ps_st,
        nc.psum_tensor("ps_b", [128, 2, 512], F32) as ps_b,
        nc.Block() as blk,
    ):
        @blk.sync
        def _(e):
            env.waits(e, B)
            for i in range(NTT):
                if i >= 1:
                    e.wait_ge(env.sem["dve"], D + 7 * (i - 1) + 7)
                env.inc("qs", e.dma_start(out=xt[:, :, :], in_=x_r[:, :, i * 512:(i + 1) * 512]), 16)

        @blk.vector
        def _(e):
            env.waits(e, B)
            env.inc("dve", e.memset(onesb[:, :], 1.0))
            env.inc("dve", e.memset(onesf[:, :], 1.0))
            for i in range(NTT):
                e.wait_ge(env.sem["qs"], QS + 16 * (i + 1))
                env.inc("dve", e.tensor_copy(xb[:, :, :], xt[:, :, :]))
                e.wait_ge(env.sem["act"], A + 5 * i + 4)
                env.inc("dve", e.tensor_tensor(ms[:, 1, :], st0[:, :], ms[:, 1, :], ALU.subtract))
                e.wait_ge(env.sem["dve"], D + 7 * i + 2)
                env.inc("dve", e.tensor_scalar(ms[:, 1, :], ms[:, 1, :], 1e-5, None, ALU.add))
                e.wait_ge(env.sem["act"], A + 5 * i + 5)
                env.inc("dve", e.reciprocal(ms[:, 1, :], st0[:, :]))
                e.wait_ge(env.sem["pe"], P + 3 * i + 3)
                env.inc("dve", e.tensor_tensor(
                    xt[:, :, :], xt[:, :, :],
                    bass.AP(ps_b, 0, [[1024, 128], [0, 8], [1, 512]]), ALU.subtract))
                e.wait_ge(env.sem["dve"], D + 7 * i + 5)
                if i >= 2:
                    env.w(e, f"lnO{i % 2}", 16 * (i // 2))
                env.inc("dve", e.tensor_tensor(
                    xo[:, i % 2, :, :], xt[:, :, :],
                    bass.AP(ps_b, 512, [[1024, 128], [0, 8], [1, 512]]), ALU.mult))
                e.wait_ge(env.sem["dve"], D + 7 * i + 6)
                if i >= 2:
                    env.w(e, f"lnP{i % 2}", 16 * (i // 2))
                env.inc("dve", e.tensor_tensor(
                    xo32[:, i % 2, :, :], xt[:, :, :],
                    bass.AP(ps_b, 512, [[1024, 128], [0, 8], [1, 512]]), ALU.mult))

        @blk.scalar
        def _(e):
            env.waits(e, B)
            for i in range(NTT):
                e.wait_ge(env.sem["qs"], QS + 16 * (i + 1))
                env.inc("act", e.square(x2[:, :, :], xt[:, :, :]))
                e.wait_ge(env.sem["pe"], P + 3 * i + 1)
                env.inc("act", e.mul(ms[:, 0, :], ps_st[:, 0, :], 1.0 / HID))
                e.wait_ge(env.sem["act"], A + 5 * i + 2)
                env.inc("act", e.square(ms[:, 1, :], ms[:, 0, :]))
                e.wait_ge(env.sem["pe"], P + 3 * i + 2)
                env.inc("act", e.mul(st0[:, :], ps_st[:, 1, :], 1.0 / HID))
                e.wait_ge(env.sem["dve"], D + 7 * i + 3)
                env.inc("act", e.sqrt(st0[:, :], ms[:, 1, :]))

        @blk.tensor
        def _(e):
            env.waits(e, B)
            for i in range(NTT):
                e.wait_ge(env.sem["dve"], D + 7 * i + 1)
                if i >= 1:
                    e.wait_ge(env.sem["act"], A + 5 * (i - 1) + 2)
                for k in range(8):
                    mm = e.matmul(ps_st[:, 0, :], onesb[:, :], xb[:, k, :], start=(k == 0), stop=(k == 7))
                env.inc("pe", e.drain())
                e.wait_ge(env.sem["act"], A + 5 * i + 1)
                if i >= 1:
                    e.wait_ge(env.sem["act"], A + 5 * (i - 1) + 4)
                for k in range(8):
                    mm = e.matmul(ps_st[:, 1, :], onesb[:, :], x2[:, k, :], start=(k == 0), stop=(k == 7))
                env.inc("pe", e.drain())
                e.wait_ge(env.sem["act"], A + 5 * i + 2)
                e.wait_ge(env.sem["dve"], D + 7 * i + 4)
                if i >= 1:
                    e.wait_ge(env.sem["dve"], D + 7 * (i - 1) + 7)
                e.matmul(ps_b[:, 0, :], onesf[:, :], ms[:, 0, :], start=True, stop=True)
                e.matmul(ps_b[:, 1, :], onesf[:, :], ms[:, 1, :], start=True, stop=True)
                env.inc("pe", e.drain())

        @blk.gpsimd
        def _(e):
            env.waits(e, B)
            for i in range(NTT):
                e.wait_ge(env.sem["dve"], D + 7 * i + 6)
                env.inc(f"lnO{i % 2}", e.dma_start(out=xn_r[:, :, i * 512:(i + 1) * 512], in_=xo[:, i % 2, :, :]), 16)
                e.wait_ge(env.sem["dve"], D + 7 * i + 7)
                env.inc(f"lnP{i % 2}", e.dma_start(out=xn32_r[:, :, i * 512:(i + 1) * 512], in_=xo32[:, i % 2, :, :]), 16)

    env.release(mk)


def st_mam_proj(env):
    nc = env.nc
    B = dict(env.cnt)
    T = env.t
    mk = env.mark()
    xn = env.alloc("mp_xn", (128, 8, TF), BF16)
    wbuf = env.alloc("mp_w", (128, 2, 8, 128), F32)
    wb = env.alloc("mp_wb", (128, 2, 8, 128), BF16)
    ev = env.alloc("mp_ev", (128, 2, 4, 512), BF16)
    evc = env.alloc("mp_evc", (128, 2, 4), BF16)
    bin_t = env.alloc("mp_bin", (128, 35), F32)
    ctx_t = env.alloc("mp_ctx", (128, 1), F32)
    dtb_t = env.alloc("mp_dtb", (32, 1), F32)
    alog_t = env.alloc("mp_alog", (32, 1), F32)
    dtraw = env.alloc("mp_dtraw", (32, 2048), F32)
    ut = env.alloc("mp_ut", (32, 2048), F32)
    mt = env.alloc("mp_mt", (32, 2048), F32)
    lt = env.alloc("mp_lt", (32, 2048), F32)
    a_t = env.alloc("mp_at", (32, 1), F32)
    acum = env.alloc("mp_acum", (32, 2048), F32)
    zt = env.alloc("mp_z", (32, 256), F32)

    w_r = T["w_in_t"].ap().rearrange("(c p) m -> p c m", p=128)
    xn_r = T["xn_s"].ap().rearrange("(c p) t -> p c t", p=128)
    CO = 35
    cls_of = lambda co: "g" if co < 16 else ("h" if co < 34 else "d")
    QS, QP, P, D, A = B["qs"], B["qp"], B["pe"], B["dve"], B["act"]

    dve_ops = [("cast", 0), ("cast", 1)]
    for co in range(CO):
        if cls_of(co) == "h":
            for tt in range(4):
                dve_ops.append(("evac", co, tt))
            dve_ops.append(("ctx", co - 16))
        if co + 2 < CO:
            dve_ops.append(("cast", co + 2))
    ord_d = {op: i + 1 for i, op in enumerate(dve_ops)}
    act_ops = []
    for co in range(CO):
        if cls_of(co) in "gd":
            for tt in range(4):
                act_ops.append(("evac", co, tt))
    ord_a = {op: i + 1 for i, op in enumerate(act_ops)}
    st_ops = []
    for co in range(CO):
        if cls_of(co) in "gh":
            for tt in range(4):
                st_ops.append(("st", co, tt))
    for j in range(18):
        st_ops.append(("stc", j))
    ord_s = {op: i + 1 for i, op in enumerate(st_ops)}
    NCTX = 18
    pe_ops = []
    for co in range(CO):
        pe_ops.append(("main", co, 0))
        pe_ops.append(("main", co, 1))
        if cls_of(co) == "h":
            pe_ops.append(("ctx", co - 16))
    ord_p = {op: i + 1 for i, op in enumerate(pe_ops)}

    with (
        nc.psum_tensor("mp_pp0", [128, 2, 512], F32) as pp0,
        nc.psum_tensor("mp_pp1", [128, 2, 512], F32) as pp1,
        nc.psum_tensor("mp_pc0", [128, 4], F32) as pc0,
        nc.psum_tensor("mp_pc1", [128, 4], F32) as pc1,
        nc.Block() as blk,
    ):
        pps = [pp0, pp1]
        pcs = [pc0, pc1]
        def ppt(co, tt):
            return pps[(2 * co + tt // 2) % 2][:, tt % 2, :]
        @blk.sync
        def _(e):
            env.waits(e, B)
            env.inc("mA", e.dma_start(out=bin_t[:, :], in_=T["b_in_sb"].ap()), 16)
            env.inc("mA", e.dma_start(out=ctx_t[:, :], in_=T["ctx_sb"].ap()), 16)
            env.inc("mA", e.dma_start(out=dtb_t[:, :], in_=T["dtb_sb"].ap()), 16)
            env.inc("mA", e.dma_start(out=alog_t[:, :], in_=T["alog_sb"].ap()), 16)
            env.inc("mX", e.dma_start(out=xn[:, :, :], in_=xn_r), 16)
            for co in range(CO):
                if co >= 2:
                    e.wait_ge(env.sem["dve"], D + ord_d[("cast", co - 2)])
                env.inc(f"mW{co % 2}", e.dma_start(out=wbuf[:, co % 2, :, :], in_=w_r[:, :, co * 128:(co + 1) * 128]), 16)

        @blk.vector
        def _(e):
            env.waits(e, B)
            def cast(co):
                env.w(e, f"mW{co % 2}", 16 * (co // 2 + 1))
                if co >= 2:
                    last = ("ctx", co - 18) if cls_of(co - 2) == "h" else ("main", co - 2, 1)
                    e.wait_ge(env.sem["pe"], P + ord_p[last])
                env.inc("dve", e.tensor_copy(wb[:, co % 2, :, :], wbuf[:, co % 2, :, :]))
            cast(0); cast(1)
            for co in range(CO):
                if cls_of(co) == "h":
                    for tt in range(4):
                        e.wait_ge(env.sem["pe"], P + ord_p[("main", co, tt // 2)])
                        env.w(e, "mA", 64)
                        if co >= 2:
                            env.w(e, f"mE{co % 2}{tt}", 16 * (co // 2))
                        env.inc("dve", e.tensor_scalar(
                            ev[:, co % 2, tt, :], ppt(co, tt),
                            bin_t[:, co:co + 1], None, ALU.add))
                    j = co - 16
                    e.wait_ge(env.sem["pe"], P + ord_p[("ctx", j)])
                    if j >= 2:
                        env.w(e, f"mC{j % 2}", 16 * (j // 2))
                    env.inc("dve", e.tensor_scalar(
                        evc[:, j % 2, 0:3], pcs[j % 2][:, 0:3],
                        bin_t[:, 16 + j:17 + j], ctx_t[:, :1], ALU.add, ALU.mult))
                if co + 2 < CO:
                    cast(co + 2)
            dlast = D + len(dve_ops)
            alast = A + len(act_ops)
            e.wait_ge(env.sem["act"], alast)
            env.inc("dve", e.tensor_scalar(dtraw[:, :], dtraw[:, :], bin_t[0:32, 34:35], dtb_t[:, :1], ALU.add, ALU.add))
            e.wait_ge(env.sem["dve"], dlast + 1)
            env.inc("dve", e.tensor_scalar_max(mt[:, :], dtraw[:, :], 0.0))
            e.wait_ge(env.sem["act"], alast + 3)
            env.inc("dve", e.tensor_scalar(ut[:, :], ut[:, :], 1.0, None, ALU.add))
            e.wait_ge(env.sem["act"], alast + 4)
            env.inc("dve", e.tensor_tensor(dtraw[:, :], mt[:, :], lt[:, :], ALU.add))
            e.wait_ge(env.sem["dve"], dlast + 4)
            env.inc("dve", e.tensor_scalar(mt[:, :], dtraw[:, :], a_t[:, :1], -1.0, ALU.mult, ALU.mult))
            env.inc("dve", e.memset(zt[:, :], 0.0))
            e.wait_ge(env.sem["dve"], dlast + 6)
            for c in range(8):
                env.inc("dve", e.tensor_tensor_scan(
                    acum[:, c * 256:(c + 1) * 256], mt[:, c * 256:(c + 1) * 256],
                    zt[:, :], 0.0, ALU.add, ALU.add))

        @blk.scalar
        def _(e):
            env.waits(e, B)
            for co in range(CO):
                if cls_of(co) == "g":
                    for tt in range(4):
                        e.wait_ge(env.sem["pe"], P + ord_p[("main", co, tt // 2)])
                        env.w(e, "mA", 64)
                        if co >= 2 and cls_of(co - 2) == "g":
                            env.w(e, f"mE{co % 2}{tt}", 16 * (co // 2))
                        env.inc("act", e.activation(ev[:, co % 2, tt, :], ppt(co, tt),
                                                    AF.Silu, bias=bin_t[:, co:co + 1]))
                elif cls_of(co) == "d":
                    for tt in range(4):
                        e.wait_ge(env.sem["pe"], P + ord_p[("main", co, tt // 2)])
                        env.inc("act", e.copy(dtraw[0:32, tt * 512:(tt + 1) * 512], ppt(co, tt)[0:32, :]))
            dlast = D + len(dve_ops)
            alast = A + len(act_ops)
            env.w(e, "mA", 64)
            env.inc("act", e.activation(a_t[:, :], alog_t[:, :], AF.Exp))
            e.wait_ge(env.sem["dve"], dlast + 1)
            env.inc("act", e.activation(ut[:, :], dtraw[:, :], AF.Abs))
            e.wait_ge(env.sem["act"], alast + 2)
            env.inc("act", e.activation(ut[:, :], ut[:, :], AF.Exp, scale=-1.0))
            e.wait_ge(env.sem["dve"], dlast + 3)
            env.inc("act", e.activation(lt[:, :], ut[:, :], AF.Ln))

        @blk.tensor
        def _(e):
            env.waits(e, B)
            env.w(e, "mX", 16)
            for co in range(CO):
                e.wait_ge(env.sem["dve"], D + ord_d[("cast", co)])
                for half in range(2):
                    hg = 2 * co + half
                    if hg >= 2:
                        if cls_of(co - 1) == "h":
                            e.wait_ge(env.sem["dve"], D + ord_d[("evac", co - 1, 2 * half + 1)])
                        else:
                            e.wait_ge(env.sem["act"], A + ord_a[("evac", co - 1, 2 * half + 1)])
                    for k in range(8):
                        for tx in range(2):
                            tt = 2 * half + tx
                            mm = e.matmul(pps[hg % 2][:, tx, :], wb[:, co % 2, k, :],
                                          xn[:, k, (tt + 1) * 512:(tt + 2) * 512],
                                          start=(k == 0), stop=(k == 7))
                    env.inc("pe", mm)
                if cls_of(co) == "h":
                    j = co - 16
                    if j >= 2:
                        e.wait_ge(env.sem["dve"], D + ord_d[("ctx", j - 2)])
                    for k in range(8):
                        mm = e.matmul(pcs[j % 2][:, 0:3], wb[:, co % 2, k, :], xn[:, k, 509:512],
                                      start=(k == 0), stop=(k == 7))
                    env.inc("pe", mm)

        @blk.gpsimd
        def _(e):
            env.waits(e, B)
            gate_r = T["gate_s"].ap()
            hbc_r = T["hbc_s"].ap()
            for co in range(CO):
                if cls_of(co) == "g":
                    for tt in range(4):
                        e.wait_ge(env.sem["act"], A + ord_a[("evac", co, tt)])
                        env.inc(f"mE{co % 2}{tt}", e.dma_start(
                            out=gate_r[co * 128:(co + 1) * 128, tt * 512:(tt + 1) * 512],
                            in_=ev[:, co % 2, tt, :]), 16)
                elif cls_of(co) == "h":
                    for tt in range(4):
                        e.wait_ge(env.sem["dve"], D + ord_d[("evac", co, tt)])
                        env.inc(f"mE{co % 2}{tt}", e.dma_start(
                            out=hbc_r[(co - 16) * 128:(co - 15) * 128, 512 + tt * 512:512 + (tt + 1) * 512],
                            in_=ev[:, co % 2, tt, :]), 16)
                    j = co - 16
                    e.wait_ge(env.sem["dve"], D + ord_d[("ctx", j)])
                    env.inc(f"mC{j % 2}", e.dma_start(
                        out=hbc_r[j * 128:(j + 1) * 128, 509:512], in_=evc[:, j % 2, 0:3]), 16)
            dlast = D + len(dve_ops)
            e.wait_ge(env.sem["dve"], dlast + 4)
            env.inc("qp", e.dma_start(out=T["dt_s"].ap(), in_=dtraw[:, :]), 16)
            e.wait_ge(env.sem["dve"], dlast + 14)
            env.inc("qp", e.dma_start(out=T["acum_s"].ap(), in_=acum[:, :]), 16)

    env.release(mk)


STAGES = {}
STAGES["ln1"] = (["x_fm"], ["xn_s", "xn32_s"], st_ln1)
STAGES["mam_proj"] = (["xn_s", "w_in_t", "b_in_sb", "ctx_sb", "dtb_sb", "alog_sb"],
                      ["gate_s", "hbc_s", "dt_s", "acum_s"], st_mam_proj)

def st_mam_p1(env):
    nc = env.nc
    B = dict(env.cnt)
    T = env.t
    mk = env.mark()
    S = Sched(env, B)
    A = S.add

    convw = env.alloc("p1_convw", (128, 72), F32)
    dexp = env.alloc("p1_dexp", (128, 16), F32)
    triu = env.alloc("p1_triu", (128, 2, 256), BF16)
    identb = env.alloc("p1_idb", (128, 128), BF16)
    identf = env.alloc("p1_idf", (128, 128), F32)
    onesb1 = env.alloc("p1_ob1", (1, 128), BF16)
    onesf1 = env.alloc("p1_of1", (1, 128), F32)
    hbc = env.alloc("p1_hbc", (128, 2, 18, 260), BF16)
    dtac = env.alloc("p1_dtac", (96, 2, 256), F32)
    convt = env.alloc("p1_conv", (128, 18, 256), F32)
    hsil = env.alloc("p1_hsil", (128, 18, 256), BF16)
    tokmh = env.alloc("p1_tokmh", (128, 2, 2, 96), BF16)
    tokml = env.alloc("p1_tokml", (128, 2, 2, 96), BF16)
    tokm = env.alloc("p1_tokm", (128, 2, 2, 96), F32)
    hib = env.alloc("p1_hib", (96, 2, 256), BF16)
    lob = env.alloc("p1_lob", (96, 2, 256), BF16)
    htraw = env.alloc("p1_htraw", (128, 2, 2, 2048), BF16)
    htok = env.alloc("p1_htok", (128, 2, 2048), BF16)
    hdd = env.alloc("p1_hdd", (128, 2, 2048), BF16)
    btok = env.alloc("p1_btok", (128, 2, 128), BF16)
    gmm = env.alloc("p1_gmm", (128, 2, 256), BF16)
    dfs = env.alloc("p1_dfs", (128, 2, 256), F32)
    ets = env.alloc("p1_ets", (128, 2, 256), BF16)
    mts = env.alloc("p1_mts", (128, 2, 256), BF16)
    ypc = env.alloc("p1_ypc", (128, 2, 16, 256), BF16)
    srun = env.alloc("p1_srun", (128, 2048), F32)
    stmp = env.alloc("p1_stmp", (128, 2048), F32)
    acrow = env.alloc("p1_acrow", (1, 2, 32, 256), F32)
    dtrow = env.alloc("p1_dtrow", (1, 2, 32), F32)
    dtrowb = env.alloc("p1_dtrowb", (1, 32), BF16)
    dtrowe = env.alloc("p1_dtrowe", (1, 2048), BF16)
    chds = env.alloc("p1_chds", (32, 8), F32)
    chdac = env.alloc("p1_chdac", (32, 1), F32)
    chrow = env.alloc("p1_chrow", (1, 32), F32)
    chrowe = env.alloc("p1_chrowe", (1, 2048), F32)

    hbc_r = T["hbc_s"].ap().rearrange("(i p) t -> p i t", p=128)
    yp_r = T["ypart_s"].ap().rearrange("(j p) t -> p j t", p=128)

    def dep(*keys):
        out = []
        for k in keys:
            if k is None:
                continue
            if k in S.keyinfo:
                out.append(k)
            else:
                assert any(isinstance(x, int) and x < 0 for x in k), f"missing dep {k}"
        return out

    with (
        nc.psum_tensor("p1_psC0", [128, 512], F32) as psC0,
        nc.psum_tensor("p1_psC1", [128, 512], F32) as psC1,
        nc.psum_tensor("p1_psD0", [128, 256], F32) as psD0,
        nc.psum_tensor("p1_psD1", [128, 256], F32) as psD1,
        nc.psum_tensor("p1_psE0", [128, 256], F32) as psE0,
        nc.psum_tensor("p1_psE1", [128, 256], F32) as psE1,
    ):
        psD = [psD0, psD1]
        psE = [psE0, psE1]
        for i, (dst, src) in enumerate([
            (convw, "convw_sb"), (dexp, "dexp_sb"), (identb, "ident_b"), (identf, "ident_f")]):
            A("sync", ("LA", i), [], lambda e, d=dst, sname=src: e.dma_start(out=d[:, :], in_=T[sname].ap()), dma_sem="pA")
        A("sync", ("LA", 4), [], lambda e: e.dma_start(
            out=triu[:, :, :], in_=T["triu_b"].ap().rearrange("(sh p) l -> p sh l", p=128)), dma_sem="pA")
        LA = ("LA", 4)
        A("vector", ("sz",), [], lambda e: e.memset(srun[:, :], 0.0))

        A("vector", ("ob1",), [], lambda e: e.memset(onesb1[:, :], 1.0))
        A("vector", ("of1",), [], lambda e: e.memset(onesf1[:, :], 1.0))

        for c in range(NCH):
            sl = c % 2
            c0 = RB + c * 256
            A("sync", ("Lh", c), dep(("cv", c - 2, 17, 3)),
              lambda e, sl=sl, c0=c0: e.dma_start(out=hbc[:, sl, :, 0:259], in_=hbc_r[:, :, c0 - 3:c0 + 256]),
              dma_sem=f"ph{sl}")
            A("sync", ("Ld1", c), dep(("hl2", c - 2)),
              lambda e, sl=sl, c=c: e.dma_start(out=dtac[0:32, sl, :], in_=T["dt_s"].ap()[:, c * 256:(c + 1) * 256]),
              dma_sem=f"pd{sl}")
            A("sync", ("Ld2", c), dep(("hl2", c - 2), ("chc", c - 2), ("dte", c - 2)),
              lambda e, sl=sl, c=c: e.dma_start(out=dtac[32:64, sl, :], in_=T["acum_s"].ap()[:, c * 256:(c + 1) * 256]),
              dma_sem=f"pe{sl}d")
            A("sync", ("Lr", c), dep(("acb", c - 2, 31)),
              lambda e, sl=sl, c=c: e.dma_start(out=acrow[0:1, sl, :, :],
                                                in_=T["acum_s"].ap()[:, c * 256:(c + 1) * 256].unsqueeze(0)),
              dma_sem=f"pr{sl}")
            A("sync", ("Lt", c), dep(("dte", c - 2)),
              lambda e, sl=sl, c=c: e.dma_start(out=dtrow[0:1, sl, :],
                                                in_=T["acum_s"].ap()[:, c * 256 + 255:c * 256 + 256].transpose([1, 0]),
                                                allow_slow_non_contiguous=True),
              dma_sem=f"pt{sl}")
            for i in range(18):
                A("vector", ("cv", c, i, 0),
                  dep(("Lh", c), ("sil", c - 1), LA) if i == 0 else dep(("Lh", c), ("sil", c - 1), ("cv", c, i - 1, 3), LA),
                  lambda e, sl=sl, i=i: e.tensor_scalar(
                      convt[:, i, :], hbc[:, sl, i, 0:256], convw[:, 4 * i:4 * i + 1], None, ALU.mult))
                for tp in range(1, 4):
                    A("vector", ("cv", c, i, tp), dep(("cv", c, i, tp - 1)),
                      lambda e, sl=sl, i=i, tp=tp: e.scalar_tensor_tensor(
                          convt[:, i, :], hbc[:, sl, i, tp:tp + 256], convw[:, 4 * i + tp:4 * i + tp + 1],
                          convt[:, i, :], ALU.mult, ALU.add))
            A("scalar", ("sil", c),
              dep(("cv", c, 17, 3), ("gmt", c - 1, 1), ("ype", c - 1, 15), ("Sc", c - 1),
                  ("Tth", c - 1, 15), ("Ttb", c - 1)),
              lambda e: e.activation(hsil[:, :, :], convt[:, :, :], AF.Silu))
            A("vector", ("dec", c), dep(("Ld1", c), ("Ld2", c), ("hl2", c - 2)),
              lambda e, sl=sl: e.tensor_scalar(
                  dtac[64:96, sl, :], dtac[32:64, sl, :], dtac[32:64, sl, 255:256], -1.0,
                  ALU.subtract, ALU.mult))
            A("scalar", ("dexp", c), dep(("dec", c)),
              lambda e, sl=sl: e.activation(dtac[64:96, sl, :], dtac[64:96, sl, :], AF.Exp))
            A("vector", ("hl1", c), dep(("dexp", c), ("Tts", c - 2, 1)),
              lambda e, sl=sl: e.tensor_copy(hib[:, sl, :], dtac[0:96, sl, :]))
            A("vector", ("hl2", c), dep(("hl1", c), ("Tts", c - 2, 1)),
              lambda e, sl=sl: e.tensor_tensor(lob[:, sl, :], dtac[0:96, sl, :], hib[:, sl, :], ALU.subtract))
            A("sync", ("Tts", c, 0), dep(("hl1", c), ("tka", c - 2)),
              lambda e, sl=sl: e.dma_start_transpose(
                  bass.AP(tokmh, sl * 192, [[384, 128], [96, 2], [1, 96]]), hib[0:96, sl, :]),
              dma_sem=f"tt{sl}")
            A("sync", ("Tts", c, 1), dep(("hl2", c), ("tka", c - 2)),
              lambda e, sl=sl: e.dma_start_transpose(
                  bass.AP(tokml, sl * 192, [[384, 128], [96, 2], [1, 96]]), lob[0:96, sl, :]),
              dma_sem=f"tt{sl}")
            A("vector", ("tka", c), dep(("Tts", c, 1), ("df", c - 2, 31, 1), ("hsc", c - 2, 1), ("hdd2", c - 2, 1)),
              lambda e, sl=sl: e.tensor_tensor(tokm[:, sl, :, :], tokmh[:, sl, :, :],
                                               tokml[:, sl, :, :], ALU.add))
            for i in range(16):
                A("sync", ("Tth", c, i), dep(("sil", c), ("hsc", c - 2, 1), ("hdd2", c - 2, 1)),
                  lambda e, sl=sl, i=i: e.dma_start_transpose(
                      bass.AP(htraw, sl * 4096 + i * 128, [[8192, 128], [2048, 2], [1, 128]]),
                      hsil[:, i, :]),
                  dma_sem=f"th{sl}")
            A("sync", ("Ttb", c), dep(("sil", c), ("stm", c - 1, 3)),
              lambda e: e.dma_start_transpose(
                  bass.AP(btok, 0, [[256, 128], [128, 2], [1, 128]]), hsil[:, 16, :]),
              dma_sem=f"tb{sl}")
            for th in range(2):
                A("vector", ("hsc", c, th),
                  dep(("Tth", c, 15), ("tka", c), ("ym", c - 1, 31, 1)),
                  lambda e, sl=sl, th=th: e.tensor_tensor(
                      bass.AP(htok, th * 2048, [[4096, 128], [64, 32], [1, 64]]),
                      bass.AP(htraw, sl * 4096 + th * 2048, [[8192, 128], [64, 32], [1, 64]]),
                      bass.AP(tokm, sl * 192 + th * 96, [[384, 128], [1, 32], [0, 64]]),
                      ALU.mult))
                A("vector", ("hdd2", c, th), dep(("hsc", c, th), ("stm", c - 1, 3)),
                  lambda e, sl=sl, th=th: e.tensor_tensor(
                      bass.AP(hdd, th * 2048, [[4096, 128], [64, 32], [1, 64]]),
                      bass.AP(htok, th * 2048, [[4096, 128], [64, 32], [1, 64]]),
                      bass.AP(tokm, sl * 192 + th * 96 + 64, [[384, 128], [1, 32], [0, 64]]),
                      ALU.mult))
            A("gpsimd", ("Sc", c), dep(("sil", c)),
              lambda e, c=c: e.dma_start(out=T["cfm_s"].ap()[:, c * 256:(c + 1) * 256], in_=hsil[:, 17, :]),
              dma_sem=f"pc{sl}")
            for sh in range(2):
                A("tensor", ("gmt", c, sh), dep(("sil", c), ("df", c - 1, 30 + sh, 1)),
                  lambda e, sh=sh: e.matmul(psD[sh][:, :], hsil[:, 16, sh * 128:(sh + 1) * 128],
                                            hsil[:, 17, :], start=True, stop=True))
                A("vector", ("gme", c, sh), dep(("gmt", c, sh), LA, ("mt", c - 1, 31, sh)),
                  lambda e, sh=sh: e.tensor_tensor(gmm[:, sh, :], psD[sh][:, :], triu[:, sh, :], ALU.mult))
            A("gpsimd", ("Ss", c), dep(("sup", c - 1, 3, 1), ("sz",)),
              lambda e, c=c: e.dma_start(out=T["sst_s"].ap()[:, c * 2048:(c + 1) * 2048], in_=srun[:, :]),
              dma_sem=f"ps{sl}")
            A("scalar", ("dte", c), dep(("Lt", c), ("dxe", c - 1)),
              lambda e, sl=sl: e.activation(dtrowb[:, :], dtrow[0:1, sl, :], AF.Exp))
            A("vector", ("dxe", c), dep(("dte", c), ("dbc", c - 1, 3)),
              lambda e: e.tensor_copy(bass.AP(dtrowe, 0, [[2048, 1], [64, 32], [1, 64]]),
                                      bass.AP(dtrowb, 0, [[32, 1], [1, 32], [0, 64]])))
            for q in range(4):
                A("tensor", ("dbc", c, q), dep(("dxe", c), ("sup", c, q - 1, 0), ("ob1",)),
                  lambda e, q=q: e.matmul(psC1[:, :], onesb1[:, :], dtrowe[0:1, q * 512:(q + 1) * 512],
                                          start=True, stop=True))
                def emit_stm(e, q=q):
                    for h8 in range(8):
                        col = q * 512 + h8 * 64
                        for kh in range(2):
                            mm = e.matmul(psC0[:, h8 * 64:(h8 + 1) * 64], btok[:, kh, :],
                                          hdd[:, kh, col:col + 64], start=(kh == 0), stop=(kh == 1))
                    return mm
                A("tensor", ("stm", c, q),
                  dep(("Ttb", c), ("sup", c, q - 1, 1), ("hdd2", c, 1)),
                  emit_stm)
                A("vector", ("sup", c, q, 0), dep(("dbc", c, q), ("Ss", c), ("sup", c - 1, q, 1)),
                  lambda e, q=q: e.tensor_tensor(stmp[:, q * 512:(q + 1) * 512],
                                                 srun[:, q * 512:(q + 1) * 512], psC1[:, :], ALU.mult))
                A("vector", ("sup", c, q, 1), dep(("sup", c, q, 0), ("stm", c, q)),
                  lambda e, q=q: e.tensor_tensor(srun[:, q * 512:(q + 1) * 512],
                                                 stmp[:, q * 512:(q + 1) * 512], psC0[:, :], ALU.add))
            A("vector", ("chc", c), dep(("Ld2", c)),
              lambda e, sl=sl, c=c: e.tensor_copy(chds[:, c:c + 1], dtac[32:64, sl, 255:256]))
            for h in range(32):
                wdep = ("df", c, h - 2, 1) if h >= 2 else ("gme", c, h)
                A("tensor", ("acb", c, h), dep(("Lr", c), wdep, ("of1",)),
                  lambda e, sl=sl, h=h: e.matmul(psD[h % 2][:, :], onesf1[:, :], acrow[0:1, sl, h, :],
                                                 start=True, stop=True))
                for sh in range(2):
                    A("vector", ("df", c, h, sh), dep(("acb", c, h), ("ex", c, h - 1, sh), ("tka", c)),
                      lambda e, sl=sl, h=h, sh=sh: e.tensor_scalar(
                          dfs[:, sh, :], psD[h % 2][:, :], tokm[:, sl, sh, 32 + h:33 + h], 0.0,
                          ALU.subtract, ALU.min))
                    A("scalar", ("ex", c, h, sh), dep(("df", c, h, sh), ("mt", c, h - 1, sh)),
                      lambda e, sh=sh: e.activation(ets[:, sh, :], dfs[:, sh, :], AF.Exp))
                    A("vector", ("mt", c, h, sh), dep(("ex", c, h, sh), ("gme", c, sh), ("ym", c, h - 1, sh)),
                      lambda e, sh=sh: e.tensor_tensor(mts[:, sh, :], ets[:, sh, :], gmm[:, sh, :], ALU.mult))
                    ywar = ("ype", c, h // 2 - 2) if h // 2 >= 2 else ("ype", c - 1, 14 + h // 2)
                    A("tensor", ("ym", c, h, sh),
                      dep(("mt", c, h, sh), ("hsc", c, sh)) + (dep(ywar) if sh == 0 and h % 2 == 0 else []),
                      lambda e, h=h, sh=sh: e.matmul(
                          psE[(h // 2) % 2][(h % 2) * 64:(h % 2) * 64 + 64, :],
                          htok[:, sh, h * 64:(h + 1) * 64], mts[:, sh, :],
                          start=(sh == 0), stop=(sh == 1)), drain=(sh == 1))
                if h % 2 == 1:
                    j = h // 2
                    A("vector", ("ype", c, j), dep(("ym", c, h, 1), ("sil", c), LA, ("Sy", c - 2)),
                      lambda e, sl=sl, j=j: e.scalar_tensor_tensor(
                          ypc[:, sl, j, :], hsil[:, j, :], dexp[:, j:j + 1], psE[j % 2][:, :],
                          ALU.mult, ALU.add))
            A("gpsimd", ("Sy", c), dep(("ype", c, 15)),
              lambda e, sl=sl, c=c: e.dma_start(out=yp_r[:, :, c * 256:(c + 1) * 256], in_=ypc[:, sl, :, :]),
              dma_sem=f"py{sl}")
        A("gpsimd", ("Sf1",), dep(("sup", NCH - 1, 3, 1)),
          lambda e: e.dma_start(out=T["st_loc"].ap()[0:128, :], in_=srun[:, :]), dma_sem="pf1")
        A("vector", ("chsum",), dep(("chc", NCH - 1)),
          lambda e: e.tensor_reduce(chdac[:, :], chds[:, :], AX.X, ALU.add))
        A("gpsimd", ("chT0",), dep(("chsum",)),
          lambda e: e.dma_start(out=T["chtot_s"].ap(), in_=chdac[:, :]), dma_sem="pf2")
        A("gpsimd", ("chT",), dep(("chT0",)),
          lambda e: e.dma_start(out=chrow[:, :], in_=T["chtot_s"].ap().transpose([1, 0]),
                                allow_slow_non_contiguous=True), dma_sem="pf2b")
        A("vector", ("chE",), dep(("chT",)),
          lambda e: e.tensor_copy(bass.AP(chrowe, 0, [[2048, 1], [64, 32], [1, 64]]),
                                  bass.AP(chrow, 0, [[32, 1], [1, 32], [0, 64]])))
        A("gpsimd", ("Sf2",), dep(("chE",)),
          lambda e: e.dma_start(out=T["st_loc"].ap()[128:129, :], in_=chrowe[:, :]), dma_sem="pf3")
        A("gpsimd", ("Sf3",), dep(("chc", NCH - 1)),
          lambda e: e.dma_start(out=T["chd_s"].ap(), in_=chds[:, :]), dma_sem="pf4")

        with nc.Block() as blk:
            S.emit(blk)
    env.release(mk)


STAGES["mam_p1"] = (["hbc_s", "dt_s", "acum_s", "convw_sb", "dexp_sb", "triu_b",
                     "ident_b", "ident_f"],
                    ["ypart_s", "cfm_s", "sst_s", "chd_s", "chtot_s", "st_loc"], st_mam_p1)


def st_mam_cc(env):
    nc = env.nc
    B = dict(env.cnt)
    T = env.t
    cb_in = nc.dram_tensor("mcc_in", [129, INTER], F32, kind="Internal")
    cb_out = nc.dram_tensor("mcc_out", [129 * NCORES, INTER], F32, kind="Internal")
    with nc.Block() as blk:
        @blk.gpsimd
        def _(e):
            env.waits(e, B)
            env.inc("qp", e.dma_start(out=cb_in.ap(), in_=T["st_loc"].ap()), 16)
            e.wait_ge(env.sem["qp"], env.cnt["qp"])
            env.inc("cc1", e.collective_compute(
                "AllGather", ALU.bypass, replica_groups=[list(range(NCORES))],
                ins=[cb_in.ap()], outs=[cb_out.ap()]), 1)
            e.wait_ge(env.sem["cc1"], env.cnt["cc1"])
            env.inc("qp", e.dma_start(out=T["st_all"].ap(), in_=cb_out.ap()), 16)
            e.wait_ge(env.sem["qp"], env.cnt["qp"])


def st_mam_scan(env):
    nc = env.nc
    B = dict(env.cnt)
    T = env.t
    mk = env.mark()
    S = Sched(env, B)
    A = S.add
    Lc = env.alloc("sc_L", (128, 8, 2048), F32)
    ldr = env.alloc("sc_ldr", (1, 2, 2048), F32)
    ldrb = env.alloc("sc_ldrb", (1, 2, 2048), BF16)
    pref = env.alloc("sc_pref", (128, 2048), F32)
    tmps = env.alloc("sc_tmp", (128, 2048), F32)
    sstc = env.alloc("sc_sst", (128, 2, 2048), F32)
    sstg = env.alloc("sc_sstg", (128, 2, 2048), BF16)
    csel = env.alloc("sc_csel", (128, 8), F32)
    chx = env.alloc("sc_chx", (32, 8), F32)
    chi = env.alloc("sc_chi", (32, 8), F32)
    ecf = env.alloc("sc_ecf", (32, 8), F32)
    ecr = env.alloc("sc_ecr", (1, 8, 32), F32)
    ece = env.alloc("sc_ece", (1, 2048), F32)
    onesb1 = env.alloc("sc_ob1", (1, 128), BF16)
    onesf1 = env.alloc("sc_of1", (1, 128), F32)
    zt8 = env.alloc("sc_z8", (32, 8), F32)

    def dep(*keys):
        out = []
        for k in keys:
            if k is None:
                continue
            if k in S.keyinfo:
                out.append(k)
            else:
                assert any(isinstance(x, int) and x < 0 for x in k), f"missing dep {k}"
        return out

    st_all_r = T["st_all"].ap().rearrange("(c r) n -> r c n", r=129)
    with (
        nc.psum_tensor("sc_db", [128, 2048], F32) as psdb,
        nc.psum_tensor("sc_ec", [128, 2048], F32) as psec,
    ):
        A("sync", ("Ls",), [], lambda e: e.dma_start(out=Lc[:, :, :], in_=st_all_r[0:128, :, :]), dma_sem="sA1")

        A("sync", ("Lcs",), [], lambda e: e.dma_start(out=csel[:, :], in_=T["csel_sb"].ap()), dma_sem="sA2")
        A("sync", ("Lch",), [], lambda e: e.dma_start(out=chx[:, :], in_=T["chd_s"].ap()), dma_sem="sA3")
        A("vector", ("ob1",), [], lambda e: e.memset(onesb1[:, :], 1.0))
        A("vector", ("of1",), [], lambda e: e.memset(onesf1[:, :], 1.0))
        A("vector", ("z8",), [], lambda e: e.memset(zt8[:, :], 0.0))
        for cs in range(1, 8):
            A("sync", ("Ll", cs), dep(("lde", cs - 2) if cs >= 3 else None),
              lambda e, cs=cs: e.dma_start(out=ldr[0:1, cs % 2, :], in_=st_all_r[128:129, cs, :]),
              dma_sem=f"sl{cs % 2}")
            A("scalar", ("lde", cs), dep(("Ll", cs), ("db", cs - 2, 3) if cs >= 3 else None),
              lambda e, cs=cs: e.activation(ldrb[0:1, cs % 2, :], ldr[0:1, cs % 2, :], AF.Exp))
            for q in range(4):
                A("tensor", ("db", cs, q), dep(("lde", cs), ("ob1",), ("mu", cs - 1, 3) if cs >= 2 else None),
                  lambda e, cs=cs, q=q: e.matmul(psdb[:, q * 512:(q + 1) * 512], onesb1[:, :],
                                                 ldrb[0:1, cs % 2, q * 512:(q + 1) * 512], start=True, stop=True))
            for q in range(4):
                A("vector", ("mu", cs, q), dep(("db", cs, 3), ("Ls",), ("ad", cs - 1, 3) if cs >= 2 else None),
                  lambda e, cs=cs, q=q: e.tensor_tensor(
                      tmps[:, q * 512:(q + 1) * 512], Lc[:, cs - 1, q * 512:(q + 1) * 512],
                      psdb[:, q * 512:(q + 1) * 512], ALU.mult))
            for q in range(4):
                A("vector", ("ad", cs, q), dep(("mu", cs, q)),
                  lambda e, cs=cs, q=q: e.tensor_tensor(
                      Lc[:, cs, q * 512:(q + 1) * 512], tmps[:, q * 512:(q + 1) * 512],
                      Lc[:, cs, q * 512:(q + 1) * 512], ALU.add))
        A("vector", ("pz",), [], lambda e: e.memset(pref[:, :], 0.0))
        for cs in range(8):
            A("vector", ("px", cs), dep(("pz",), ("Lcs",), ("ad", cs, 3) if cs > 0 else ("Ls",), ("px", cs - 1)),
              lambda e, cs=cs: e.scalar_tensor_tensor(
                  pref[:, :], Lc[:, cs, :], csel[:, cs:cs + 1], pref[:, :], ALU.mult, ALU.add))
        A("vector", ("ci",), dep(("Lch",), ("z8",)),
          lambda e: e.tensor_tensor_scan(chi[:, :], chx[:, :], zt8[:, :], 0.0, ALU.add, ALU.add))
        A("vector", ("cx0",), dep(("ci",)), lambda e: e.memset(ecf[:, 0:1], 0.0))
        A("vector", ("cx1",), dep(("cx0",)), lambda e: e.tensor_copy(ecf[:, 1:8], chi[:, 0:7]))
        A("scalar", ("ce",), dep(("cx1",)), lambda e: e.activation(ecf[:, :], ecf[:, :], AF.Exp))
        A("gpsimd", ("cb0",), dep(("ce",)),
          lambda e: e.dma_start(out=T["ec_s"].ap(), in_=ecf[:, :]), dma_sem="sB")
        A("sync", ("cb1",), dep(("cb0",)),
          lambda e: e.dma_start(out=ecr[0:1, :, :], in_=T["ec_s"].ap().transpose([1, 0]),
                                allow_slow_non_contiguous=True), dma_sem="sB2")
        for c in range(NCH):
            sl = c % 2
            A("sync", ("Lss", c), dep(("sg", c - 2, 1)),
              lambda e, sl=sl, c=c: e.dma_start(out=sstc[:, sl, :], in_=T["sst_s"].ap()[:, c * 2048:(c + 1) * 2048]),
              dma_sem=f"ss{sl}")
            A("vector", ("ee", c), dep(("cb1",), ("eb", c - 1)),
              lambda e, c=c: e.tensor_copy(bass.AP(ece, 0, [[2048, 1], [64, 32], [1, 64]]),
                                           bass.AP(ecr, c * 32, [[256, 1], [1, 32], [0, 64]])))
            def emit_eb(e):
                for q in range(4):
                    mm = e.matmul(psec[:, q * 512:(q + 1) * 512], onesf1[:, :],
                                  ece[0:1, q * 512:(q + 1) * 512], start=True, stop=True)
                return mm
            A("tensor", ("eb", c), dep(("ee", c), ("of1",), ("sg", c - 1, 0)), emit_eb)
            A("vector", ("sg", c, 0), dep(("eb", c), ("px", 7)),
              lambda e, sl=sl: e.tensor_tensor(tmps[:, :], pref[:, :],
                                               bass.AP(psec, 0, [[2048, 128], [1, 2048]]), ALU.mult))
            A("vector", ("sg", c, 1), dep(("sg", c, 0), ("Lss", c), ("Sg", c - 2)),
              lambda e, sl=sl: e.tensor_tensor(sstg[:, sl, :], tmps[:, :], sstc[:, sl, :], ALU.add))
            A("gpsimd", ("Sg", c), dep(("sg", c, 1)),
              lambda e, sl=sl, c=c: e.dma_start(out=T["sstg_s"].ap()[:, c * 2048:(c + 1) * 2048],
                                                in_=sstg[:, sl, :]), dma_sem=f"sg{sl}")
        with nc.Block() as blk:
            S.emit(blk)
    env.release(mk)


STAGES["mam_cc"] = (["st_loc"], ["st_all"], st_mam_cc)
STAGES["mam_scan"] = (["st_all", "sst_s", "chd_s", "csel_sb"], ["sstg_s", "ec_s"], st_mam_scan)

def st_mam_p2(env):
    nc = env.nc
    B = dict(env.cnt)
    T = env.t
    mk = env.mark()
    S = Sched(env, B)
    A = S.add
    wout = env.alloc("p2_wout", (128, 16, 1024), BF16)
    wtmp = env.alloc("p2_wtmp", (128, 2, 1024), F32)
    normw = env.alloc("p2_nw", (128, 16), F32)
    boutt = env.alloc("p2_bo", (128, 8), F32)
    onesb1 = env.alloc("p2_ob1", (1, 128), BF16)
    onesf1 = env.alloc("p2_of1", (1, 128), F32)
    ob128 = env.alloc("p2_ob128", (128, 1), BF16)
    ypc = env.alloc("p2_ypc", (128, 2, 16, 256), BF16)
    sstg = env.alloc("p2_sstg", (128, 2, 2048), BF16)
    cfm = env.alloc("p2_cfm", (128, 2, 256), BF16)
    gate = env.alloc("p2_gate", (128, 2, 16, 256), BF16)
    acf = env.alloc("p2_acf", (32, 2, 256), F32)
    sdof = env.alloc("p2_sdof", (32, 2, 256), BF16)
    sdob = env.alloc("p2_sdob", (1, 2, 32, 256), BF16)
    csd = env.alloc("p2_csd", (128, 2, 256), BF16)
    yf = env.alloc("p2_yf", (128, 16, 256), F32)
    gbuf = env.alloc("p2_gbuf", (128, 16, 256), F32)
    g2 = env.alloc("p2_g2", (128, 16, 256), BF16)
    gn = env.alloc("p2_gn", (128, 16, 256), BF16)
    rrow = env.alloc("p2_rr", (1, 256), F32)
    rrow2 = env.alloc("p2_rr2", (1, 256), F32)
    att = env.alloc("p2_att", (128, 2, 8, 256), F32)

    yp_r = T["ypart_s"].ap().rearrange("(j p) t -> p j t", p=128)
    ga_r = T["gate_s"].ap().rearrange("(j p) t -> p j t", p=128)
    at_r = T["attn0_s"].ap().rearrange("(m p) t -> p m t", p=128)
    wo_r = T["w_out_t"].ap().rearrange("(k p) m -> p k m", p=128)

    def dep(*keys):
        out = []
        for k in keys:
            if k is None:
                continue
            if k in S.keyinfo:
                out.append(k)
            else:
                assert any(isinstance(x, int) and x < 0 for x in k), f"missing dep {k}"
        return out

    with (
        nc.psum_tensor("p2_psF0", [128, 256], F32) as psF0,
        nc.psum_tensor("p2_psF1", [128, 256], F32) as psF1,
        nc.psum_tensor("p2_psG0", [128, 256], F32) as psG0,
        nc.psum_tensor("p2_psG1", [128, 256], F32) as psG1,
        nc.psum_tensor("p2_psR", [128, 256], F32) as psR,
        nc.psum_tensor("p2_psO", [128, 4, 256], F32) as psO,
    ):
        psF = [psF0, psF1]
        psG = [psG0, psG1]
        A("sync", ("Ln",), [], lambda e: e.dma_start(out=normw[:, :], in_=T["normw_sb"].ap()), dma_sem="qA1")
        A("sync", ("Lb",), [], lambda e: e.dma_start(out=boutt[:, :], in_=T["bout_sb"].ap()), dma_sem="qA2")
        A("vector", ("ob1",), [], lambda e: e.memset(onesb1[:, :], 1.0))
        A("vector", ("of1",), [], lambda e: e.memset(onesf1[:, :], 1.0))
        A("vector", ("o128",), [], lambda e: e.memset(ob128[:, :], 1.0))
        for i in range(16):
            A("sync", ("Lw", i), dep(("wc", i - 2)),
              lambda e, i=i: e.dma_start(out=wtmp[:, i % 2, :], in_=wo_r[:, i, :]), dma_sem=f"qw{i % 2}")
            A("vector", ("wc", i), dep(("Lw", i), ("Ln",)),
              lambda e, i=i: e.tensor_scalar(wout[:, i, :], wtmp[:, i % 2, :], normw[:, i:i + 1], None, ALU.mult))
        for c in range(NCH):
            sl = c % 2
            t0 = c * 256
            A("sync", ("Ly", c), dep(("ye", c - 2, 15)),
              lambda e, sl=sl, t0=t0: e.dma_start(out=ypc[:, sl, :, :], in_=yp_r[:, :, t0:t0 + 256]),
              dma_sem=f"qy{sl}")
            A("sync", ("Lsg", c), dep(("yo", c - 2, 31)),
              lambda e, sl=sl, c=c: e.dma_start(out=sstg[:, sl, :], in_=T["sstg_s"].ap()[:, c * 2048:(c + 1) * 2048]),
              dma_sem=f"qg{sl}")
            A("sync", ("Lc2", c), dep(("cs", c - 2, 31)),
              lambda e, sl=sl, t0=t0: e.dma_start(out=cfm[:, sl, :], in_=T["cfm_s"].ap()[:, t0:t0 + 256]),
              dma_sem=f"qc{sl}")
            A("sync", ("La2", c), dep(("sx", c - 2)),
              lambda e, sl=sl, t0=t0: e.dma_start(out=acf[:, sl, :], in_=T["acum_s"].ap()[:, t0:t0 + 256]),
              dma_sem=f"qa{sl}")
            A("sync", ("Lg", c), dep(("gm", c - 2)),
              lambda e, sl=sl, t0=t0: e.dma_start(out=gate[:, sl, :, :], in_=ga_r[:, :, t0:t0 + 256]),
              dma_sem=f"qe{sl}")
            A("scalar", ("sx", c), dep(("La2", c), ("Sd", c - 2)),
              lambda e, sl=sl: e.activation(sdof[:, sl, :], acf[:, sl, :], AF.Exp))
            A("gpsimd", ("Sd", c), dep(("sx", c)),
              lambda e, sl=sl: e.dma_start(out=T["sdo_s"].ap()[sl * 32:(sl + 1) * 32, :], in_=sdof[:, sl, :]),
              dma_sem=f"qd{sl}")
            A("sync", ("Lsd", c), dep(("Sd", c), ("sb", c - 2, 31)),
              lambda e, sl=sl: e.dma_start(out=sdob[0:1, sl, :, :],
                                           in_=T["sdo_s"].ap()[sl * 32:(sl + 1) * 32, :].unsqueeze(0)),
              dma_sem=f"qd2{sl}")
            for h in range(32):
                A("tensor", ("sb", c, h), dep(("Lsd", c), ("ob1",), ("cs", c, h - 2)),
                  lambda e, sl=sl, h=h: e.matmul(psF[h % 2][:, :], onesb1[:, :], sdob[0:1, sl, h, :],
                                                 start=True, stop=True))
                A("vector", ("cs", c, h), dep(("sb", c, h), ("Lc2", c), ("yo", c, h - 2)),
                  lambda e, sl=sl, h=h: e.tensor_tensor(csd[:, h % 2, :], cfm[:, sl, :], psF[h % 2][:, :], ALU.mult))
                j = h // 2
                ywar = ("ye", c, j - 2) if j >= 2 else ("ye", c - 1, 14 + j)
                A("tensor", ("yo", c, h), dep(("cs", c, h), ("Lsg", c)) + (dep(ywar) if h % 2 == 0 else []),
                  lambda e, sl=sl, h=h: e.matmul(psG[(h // 2) % 2][(h % 2) * 64:(h % 2) * 64 + 64, :],
                                                 sstg[:, sl, h * 64:(h + 1) * 64], csd[:, h % 2, :],
                                                 start=True, stop=True))
                if h % 2 == 1:
                    A("vector", ("ye", c, j), dep(("yo", c, h), ("Ly", c), ("gm", c - 1)),
                      lambda e, sl=sl, j=j: e.tensor_tensor(yf[:, j, :], psG[j % 2][:, :],
                                                            ypc[:, sl, j, :], ALU.add))
            A("vector", ("gm", c), dep(("ye", c, 15), ("Lg", c), ("g2", c - 1), ("gn", c - 1)),
              lambda e, sl=sl: e.tensor_tensor(gbuf[:, :, :], yf[:, :, :], gate[:, sl, :, :], ALU.mult))
            A("scalar", ("g2", c), dep(("gm", c), ("rs", c - 1)),
              lambda e: e.activation(g2[:, :, :], gbuf[:, :, :], AF.Square))
            def emit_rs(e):
                for i in range(16):
                    mm = e.matmul(psR[0:1, :], ob128[:, :], g2[:, i, :], start=(i == 0), stop=(i == 15))
                return mm
            A("tensor", ("rs", c), dep(("g2", c), ("o128",), ("gn", c - 1)), emit_rs)
            A("scalar", ("r1", c), dep(("rs", c), ("rb", c - 1)),
              lambda e: e.mul(rrow[:, :], psR[0:1, :], 1.0 / INTER))
            A("vector", ("r2", c), dep(("r1", c)),
              lambda e: e.tensor_scalar(rrow[:, :], rrow[:, :], 1e-5, None, ALU.add))
            A("scalar", ("r3", c), dep(("r2", c), ("r4", c - 1)),
              lambda e: e.sqrt(rrow2[:, :], rrow[:, :]))
            A("vector", ("r4", c), dep(("r3", c)),
              lambda e: e.reciprocal(rrow2[:, :], rrow2[:, :]))
            A("tensor", ("rb", c), dep(("r4", c), ("of1",)),
              lambda e: e.matmul(psR[:, :], onesf1[:, :], rrow2[:, :], start=True, stop=True))
            A("vector", ("gn", c), dep(("rb", c), ("gm", c), ("om", c - 1, 1, 3)),
              lambda e: e.tensor_tensor(gn[:, :, :], gbuf[:, :, :],
                                        bass.AP(psR, 0, [[256, 128], [0, 16], [1, 256]]), ALU.mult))
            for w in range(2):
                for m4 in range(4):
                    m = w * 4 + m4
                    owar = ("oe", c, 0, 3) if w == 1 else ("oe", c - 1, 1, 3)
                    def emit_om(e, m=m, m4=m4):
                        for k in range(16):
                            mm = e.matmul(psO[:, m4, :], wout[:, k, m * 128:(m + 1) * 128], gn[:, k, :],
                                          start=(k == 0), stop=(k == 15))
                        return mm
                    A("tensor", ("om", c, w, m4), dep(("gn", c), ("wc", 15), owar), emit_om)
                for m4 in range(4):
                    m = w * 4 + m4
                    A("vector", ("oe", c, w, m4), dep(("om", c, w, 3), ("Lb",), ("So", c - 2)),
                      lambda e, sl=sl, m=m, m4=m4: e.tensor_scalar(
                          att[:, sl, m, :], psO[:, m4, :], boutt[:, m:m + 1], None, ALU.add))
            A("gpsimd", ("So", c), dep(("oe", c, 1, 3)),
              lambda e, sl=sl, t0=t0: e.dma_start(out=at_r[:, :, t0:t0 + 256], in_=att[:, sl, :, :]),
              dma_sem=f"qo{sl}")
        with nc.Block() as blk:
            S.emit(blk)
    env.release(mk)


STAGES["mam_p2"] = (["ypart_s", "sstg_s", "cfm_s", "acum_s", "gate_s", "w_out_t",
                     "normw_sb", "bout_sb"], ["attn0_s", "sdo_s"], st_mam_p2)


def _moe_experts(env, S, A, dep, T, EXN, srcfn, comb_name, hid_name, w2b_name,
                 xnr, crow, gbsb, w1f, w1b, w3f, w3b, w2f, w2b, s1b, s2b, hbuf,
                 psG, psH, onesb1, scb_dep):
    hid_r = T[hid_name].ap()
    w2s_r = T[w2b_name].ap()
    for ex in range(EXN):
        sl = ex % 2
        w1src, w3src, w2src, combrow = srcfn(ex)
        A("sync", ("Lw1", ex), dep(("cw1", ex - 2)),
          lambda e, sl=sl, w1src=w1src: e.dma_start(out=w1f[:, sl, :, :], in_=w1src), dma_sem=f"f1{sl}")
        A("vector", ("cw1", ex), dep(("Lw1", ex), ("m1", ex - 2, 3, 3)),
          lambda e, sl=sl: e.tensor_copy(w1b[:, sl, :, :], w1f[:, sl, :, :]))
        A("sync", ("Lw3", ex), dep(("cw3", ex - 1)),
          lambda e, w3src=w3src: e.dma_start(out=w3f[:, :, :], in_=w3src), dma_sem=f"f3{ex % 2}")
        A("scalar", ("cw3", ex), dep(("Lw3", ex), ("m3", ex - 2, 3, 3)),
          lambda e, sl=sl: e.copy(w3b[:, sl, :, :], w3f[:, :, :]))
        A("sync", ("Lw2", ex), dep(("cw2", ex - 1)),
          lambda e, w2src=w2src: e.dma_start(out=w2f[:, :, :], in_=w2src), dma_sem=f"f2{ex % 2}")
        A("gpsimd", ("cw2", ex), dep(("Lw2", ex), ("Sw2", ex - 1)),
          lambda e: e.tensor_copy(w2b[:, :, :], w2f[:, :, :]))
        A("gpsimd", ("Sw2", ex), dep(("cw2", ex)),
          lambda e, ex=ex: e.dma_start(
              out=w2s_r.rearrange("(x k p) m -> p x k m", p=128, k=4)[:, ex, :, :],
              in_=w2b[:, :, :]), dma_sem=f"fw{ex % 2}")
        A("sync", ("Lcr", ex), dep(("gb", ex - 1, 3), *scb_dep),
          lambda e, combrow=combrow: e.dma_start(out=crow[0:1, :], in_=combrow), dma_sem=f"fc{ex % 2}")
        for q in range(4):
            gwar = ("ge", ex, q - 1) if q >= 1 else ("ge", ex - 1, 3)
            A("tensor", ("gb", ex, q), dep(("Lcr", ex), ("ob1",), gwar),
              lambda e, q=q: e.matmul(psG[:, :], onesb1[:, :],
                                      crow[0:1, q * 512:(q + 1) * 512], start=True, stop=True))
            A("vector", ("ge", ex, q), dep(("gb", ex, q), ("s2", ex - 1, 3, 3)),
              lambda e, q=q: e.tensor_copy(gbsb[:, q * 512:(q + 1) * 512], psG[:, :]))
        for step in range(16):
            ft, tt = divmod(step, 4)
            hl = step % 2
            pft, ptt = divmod(step - 1, 4)
            wft, wtt = divmod(step - 2, 4)
            gstep = ex * 16 + step
            A("tensor", ("m1", ex, ft, tt),
              dep(("cw1", ex), ("Lx",),
                  ("s1", ex, pft, ptt) if step >= 1 else ("s1", ex - 1, 3, 3)),
              lambda e, sl=sl, ft=ft, tt=tt: [
                  e.matmul(psH[0][:, :], w1b[:, sl, k, ft * 128:(ft + 1) * 128],
                           xnr[:, k, tt * 512:(tt + 1) * 512], start=(k == 0), stop=(k == 7))
                  for k in range(8)][-1])
            A("scalar", ("s1", ex, ft, tt),
              dep(("m1", ex, ft, tt),
                  ("hm", ex, wft, wtt) if step >= 2 else ("hm", ex - 1, 3, 2 + step)),
              lambda e, hl=hl: e.activation(s1b[:, hl, :], psH[0][:, :], AF.Silu))
            A("tensor", ("m3", ex, ft, tt),
              dep(("cw3", ex), ("Lx",),
                  ("s2", ex, pft, ptt) if step >= 1 else ("s2", ex - 1, 3, 3)),
              lambda e, sl=sl, ft=ft, tt=tt: [
                  e.matmul(psH[1][:, :], w3b[:, sl, k, ft * 128:(ft + 1) * 128],
                           xnr[:, k, tt * 512:(tt + 1) * 512], start=(k == 0), stop=(k == 7))
                  for k in range(8)][-1])
            A("vector", ("s2", ex, ft, tt),
              dep(("m3", ex, ft, tt), ("ge", ex, 3),
                  ("hm", ex, wft, wtt) if step >= 2 else ("hm", ex - 1, 3, 2 + step)),
              lambda e, hl=hl, tt=tt: e.tensor_tensor(
                  s2b[:, hl, :], psH[1][:, :],
                  gbsb[:, tt * 512:(tt + 1) * 512], ALU.mult))
            A("gpsimd", ("hm", ex, ft, tt),
              dep(("s1", ex, ft, tt), ("s2", ex, ft, tt),
                  ("Sh", ex, wft, wtt) if step >= 2 else ("Sh", ex - 1, 3, 2 + step)),
              lambda e, hl=hl: e.tensor_mul(hbuf[:, hl, :], s1b[:, hl, :], s2b[:, hl, :]))
            A("gpsimd", ("Sh", ex, ft, tt), dep(("hm", ex, ft, tt)),
              lambda e, hl=hl, ex=ex, ft=ft, tt=tt: e.dma_start(
                  out=hid_r[ex * 512 + ft * 128:ex * 512 + (ft + 1) * 128,
                            tt * 512:(tt + 1) * 512],
                  in_=hbuf[:, hl, :]), dma_sem=f"fh{hl}")


def st_ffn0a(env):
    nc = env.nc
    B = dict(env.cnt)
    T = env.t
    mk = env.mark()
    S = Sched(env, B)
    A = S.add
    xnr = env.alloc("f0_xn", (128, 8, 2048), BF16)
    xg32 = env.alloc("f0_xg", (128, 8, 512), F32)
    wg0f = env.alloc("f0_wgf", (128, 8, 16), F32)
    identb = env.alloc("f0_idb", (128, 128), BF16)
    identf = env.alloc("f0_idf", (128, 128), F32)
    onesb1 = env.alloc("f0_ob1", (1, 128), BF16)
    scf = env.alloc("f0_scf", (16, 2048), F32)
    schb = env.alloc("f0_schb", (16, 2048), BF16)
    sclb = env.alloc("f0_sclb", (16, 2048), BF16)
    scth = env.alloc("f0_scth", (128, 16, 16), BF16)
    sctl = env.alloc("f0_sctl", (128, 16, 16), BF16)
    sct = env.alloc("f0_sct", (128, 16, 16), F32)
    g_mp = env.alloc("f0_mp", (128, 16, 8), F32)
    g_np = env.alloc("f0_np", (128, 16, 8), F32)
    g_x = env.alloc("f0_x", (128, 16, 4), F32)
    g_y = env.alloc("f0_y", (128, 16, 4), F32)
    g_z = env.alloc("f0_z", (128, 16, 4), F32)
    g_gs = env.alloc("f0_gs", (128, 16, 4), F32)
    g_a = env.alloc("f0_a", (128, 16, 2), F32)
    g_b = env.alloc("f0_b", (128, 16, 2), F32)
    g_s2 = env.alloc("f0_s2", (128, 16, 1), F32)
    g_al = env.alloc("f0_al", (128, 16, 4), F32)
    g_ms = env.alloc("f0_ms", (128, 16, 16), F32)
    g_m1 = env.alloc("f0_m1", (128, 16, 1), F32)
    g_mk = env.alloc("f0_mk", (128, 16, 16), F32)
    g_cr = env.alloc("f0_cr", (128, 16, 16), F32)
    g_dn = env.alloc("f0_dn", (128, 16, 1), F32)
    comb = env.alloc("f0_comb", (128, 16, 16), BF16)
    combT = env.alloc("f0_combT", (128, 2, 128), BF16)
    onerow = env.alloc("f0_onerow", (1, 2048), BF16)
    crow = env.alloc("f0_crow", (1, 2048), BF16)
    gbsb = env.alloc("f0_gbsb", (128, 2048), BF16)
    w1f = env.alloc("f0_w1f", (128, 2, 8, 512), F32)
    w1b = env.alloc("f0_w1b", (128, 2, 8, 512), BF16)
    w3f = env.alloc("f0_w3f", (128, 8, 512), F32)
    w3b = env.alloc("f0_w3b", (128, 2, 8, 512), BF16)
    w2f = env.alloc("f0_w2f", (128, 4, 1024), F32)
    w2b = env.alloc("f0_w2b", (128, 4, 1024), BF16)
    s1b = env.alloc("f0_s1", (128, 2, 512), BF16)
    s2b = env.alloc("f0_s2b", (128, 2, 512), BF16)
    hbuf = env.alloc("f0_hb", (128, 2, 512), BF16)

    xn_r = T["xn_s"].ap().rearrange("(k p) t -> p k t", p=128)
    xn32_r = T["xn32_s"].ap().rearrange("(k p) t -> p k t", p=128)
    wg_r = T["wg0_t"].ap().rearrange("(k p) m -> p k m", p=128)
    hid_r = T["hid_s"].ap()
    w2s_r = T["w2b_s"].ap()

    def dep(*keys):
        out = []
        for k in keys:
            if k is None:
                continue
            if k in S.keyinfo:
                out.append(k)
            else:
                assert any(isinstance(x, int) and x < 0 for x in k), f"missing dep {k}"
        return out

    def vw(t, off, *dims):
        ps = 1
        for d in t.shape[1:]:
            ps *= int(d)
        return bass.AP(t, off, [[ps, 128]] + [[int(a), int(b)] for a, b in dims])

    with (
        nc.psum_tensor("f0_psS", [16, 512], F32) as psS,
        nc.psum_tensor("f0_psG", [128, 512], F32) as psG,
        nc.psum_tensor("f0_psH0", [128, 512], F32) as psH0,
        nc.psum_tensor("f0_psH1", [128, 512], F32) as psH1,
    ):
        psH = [psH0, psH1]
        A("sync", ("Lx",), [], lambda e: e.dma_start(out=xnr[:, :, :], in_=xn_r[:, :, RB:]), dma_sem="fA1")
        A("sync", ("Lg",), [], lambda e: e.dma_start(out=wg0f[:, :, :], in_=wg_r), dma_sem="fA2")
        A("sync", ("Li1",), [], lambda e: e.dma_start(out=identb[:, :], in_=T["ident_b"].ap()), dma_sem="fA3")
        A("sync", ("Li2",), [], lambda e: e.dma_start(out=identf[:, :], in_=T["ident_f"].ap()), dma_sem="fA4")
        A("vector", ("ob1",), [], lambda e: e.memset(onesb1[:, :], 1.0))
        A("vector", ("or1",), [], lambda e: e.memset(onerow[:, :], 1.0))
        for tt in range(4):
            A("sync", ("Lx32", tt), dep(("sm", tt - 1)),
              lambda e, tt=tt: e.dma_start(out=xg32[:, :, :],
                                           in_=xn32_r[:, :, RB + tt * 512:RB + (tt + 1) * 512]),
              dma_sem=f"fx{tt % 2}")
            def emit_sc(e, tt=tt):
                for k in range(8):
                    mm = e.matmul(psS[:, :], wg0f[:, k, :], xg32[:, k, :],
                                  start=(k == 0), stop=(k == 7))
                return mm
            A("tensor", ("sm", tt), dep(("Lg",), ("Lx32", tt), ("se", tt - 1)), emit_sc)
            A("scalar", ("se", tt), dep(("sm", tt)),
              lambda e, tt=tt: e.activation(scf[:, tt * 512:(tt + 1) * 512], psS[:, :], AF.Sigmoid))
        A("vector", ("sh1",), dep(("se", 3)), lambda e: e.tensor_copy(schb[:, :], scf[:, :]))
        A("vector", ("sh2",), dep(("sh1",)), lambda e: e.tensor_tensor(sclb[:, :], scf[:, :], schb[:, :], ALU.subtract))
        A("sync", ("Tsh",), dep(("sh1",)),
          lambda e: e.dma_start_transpose(bass.AP(scth, 0, [[256, 128], [16, 16], [1, 16]]), schb[:, :]),
          dma_sem="fT1")
        A("sync", ("Tsl",), dep(("sh2",)),
          lambda e: e.dma_start_transpose(bass.AP(sctl, 0, [[256, 128], [16, 16], [1, 16]]), sclb[:, :]),
          dma_sem="fT2")
        A("vector", ("te", 15), dep(("Tsh",), ("Tsl",)),
          lambda e: e.tensor_tensor(sct[:, :, :], scth[:, :, :], sctl[:, :, :], ALU.add))
        L = ("te", 15)
        def TT(key, d, out, i0, i1, op):
            A("vector", key, dep(*d), lambda e, out=out, i0=i0, i1=i1, op=op: e.tensor_tensor(out, i0, i1, op))
        TT(("q1",), [L], g_mp[:, :, :], vw(sct, 0, (16, 16), (2, 8)), vw(sct, 1, (16, 16), (2, 8)), ALU.max)
        TT(("q2",), [L], g_np[:, :, :], vw(sct, 0, (16, 16), (2, 8)), vw(sct, 1, (16, 16), (2, 8)), ALU.min)
        TT(("q3",), [("q1",)], g_x[:, :, :], vw(g_mp, 0, (8, 16), (2, 4)), vw(g_mp, 1, (8, 16), (2, 4)), ALU.max)
        TT(("q4",), [("q1",)], g_y[:, :, :], vw(g_mp, 0, (8, 16), (2, 4)), vw(g_mp, 1, (8, 16), (2, 4)), ALU.min)
        TT(("q5",), [("q2",)], g_z[:, :, :], vw(g_np, 0, (8, 16), (2, 4)), vw(g_np, 1, (8, 16), (2, 4)), ALU.max)
        TT(("q6",), [("q4",), ("q5",)], g_y[:, :, :], g_y[:, :, :], g_z[:, :, :], ALU.max)
        TT(("q7",), [("q3",), ("q6",)], g_gs[:, :, :], g_x[:, :, :], g_y[:, :, :], ALU.add)
        TT(("q8",), [("q7",)], g_a[:, :, :], vw(g_gs, 0, (4, 16), (2, 2)), vw(g_gs, 1, (4, 16), (2, 2)), ALU.max)
        TT(("q9",), [("q7",)], g_b[:, :, :], vw(g_gs, 0, (4, 16), (2, 2)), vw(g_gs, 1, (4, 16), (2, 2)), ALU.min)
        TT(("qa",), [("q8",)], g_s2[:, :, :], vw(g_a, 0, (2, 16), (1, 1)), vw(g_a, 1, (2, 16), (1, 1)), ALU.min)
        TT(("qb",), [("q9",)], g_z[:, :, 0:1], vw(g_b, 0, (2, 16), (1, 1)), vw(g_b, 1, (2, 16), (1, 1)), ALU.max)
        TT(("qc",), [("qa",), ("qb",)], g_s2[:, :, :], g_s2[:, :, :], g_z[:, :, 0:1], ALU.max)
        TT(("qd",), [("qc",), ("q7",)], g_al[:, :, :], g_gs[:, :, :], vw(g_s2, 0, (1, 16), (0, 4)), ALU.is_ge)
        A("vector", ("qe",), dep(("qd",), L),
          lambda e: e.scalar_tensor_tensor(vw(g_ms, 0, (16, 16), (4, 4), (1, 4)),
                                           vw(g_al, 0, (4, 16), (1, 4), (0, 4)), BIG,
                                           vw(sct, 0, (16, 16), (4, 4), (1, 4)), ALU.mult, ALU.add))
        for r in range(3):
            A("vector", ("r", r, 0), dep(("qe",) if r == 0 else ("r", r - 1, 2)),
              lambda e: e.tensor_reduce(g_m1[:, :, 0], g_ms[:, :, :], AX.X, ALU.max))
            TT(("r", r, 1), [("r", r, 0)], g_mk[:, :, :], g_ms[:, :, :], vw(g_m1, 0, (1, 16), (0, 16)), ALU.is_ge)
            A("vector", ("r", r, 2), dep(("r", r, 1)),
              lambda e: e.scalar_tensor_tensor(g_ms[:, :, :], g_mk[:, :, :], -BIG, g_ms[:, :, :],
                                               ALU.mult, ALU.add))
        A("vector", ("tau",), dep(("r", 2, 2)),
          lambda e: e.tensor_reduce(g_m1[:, :, 0], g_ms[:, :, :], AX.X, ALU.max))
        A("vector", ("qe2",), dep(("tau",), ("qd",)),
          lambda e: e.scalar_tensor_tensor(vw(g_mk, 0, (16, 16), (4, 4), (1, 4)),
                                           vw(g_al, 0, (4, 16), (1, 4), (0, 4)), BIG,
                                           vw(sct, 0, (16, 16), (4, 4), (1, 4)), ALU.mult, ALU.add))
        TT(("sel",), [("qe2",)], g_mk[:, :, :], g_mk[:, :, :], vw(g_m1, 0, (1, 16), (0, 16)), ALU.is_ge)
        TT(("crw",), [("sel",)], g_cr[:, :, :], g_mk[:, :, :], sct[:, :, :], ALU.mult)
        A("vector", ("den",), dep(("crw",)),
          lambda e: e.tensor_reduce(g_dn[:, :, 0], g_cr[:, :, :], AX.X, ALU.add))
        A("vector", ("den2",), dep(("den",)),
          lambda e: e.tensor_scalar(g_dn[:, :, :], g_dn[:, :, :], 1e-6, None, ALU.add))
        A("vector", ("rec",), dep(("den2",)), lambda e: e.reciprocal(g_dn[:, :, :], g_dn[:, :, :]))
        TT(("cmb",), [("rec",), ("crw",)], comb[:, :, :], g_cr[:, :, :], vw(g_dn, 0, (1, 16), (0, 16)), ALU.mult)
        for hf in range(2):
            A("sync", ("Ttb2", hf), dep(("cmb",)),
              lambda e, hf=hf: e.dma_start_transpose(
                  bass.AP(combT, hf * 128, [[256, 128], [1, 128]]),
                  bass.AP(comb, hf * 128, [[256, 128], [1, 128]])),
              dma_sem=f"fU{hf}")
        for hf in range(2):
            A("gpsimd", ("Scb",) if hf == 1 else ("Scb0",), dep(("Ttb2", hf)),
              lambda e, hf=hf: e.dma_start(
                  out=bass.AP(T["comb_s"], hf * 1024, [[128, 8], [2048, 16], [1, 128]]),
                  in_=combT[:, hf, :]), dma_sem="fB1")
        A("gpsimd", ("Sor", 0), dep(("or1",)),
          lambda e: e.dma_start(out=T["comb_s"].ap()[16:17, :], in_=onerow[:, :]), dma_sem="fB2")
        A("gpsimd", ("Sor", 1), dep(("Sor", 0)),
          lambda e: e.dma_start(out=T["comb_s"].ap()[17:18, :], in_=onerow[:, :]), dma_sem="fB2")
        def srcfn(ex):
            if ex < 16:
                return (T["w1e_t"].ap()[ex].rearrange("(k p) f -> p k f", p=128),
                        T["w3e_t"].ap()[ex].rearrange("(k p) f -> p k f", p=128),
                        T["w2e_t"].ap()[ex].rearrange("(k p) m -> p k m", p=128),
                        T["comb_s"].ap()[ex:ex + 1, :])
            fh = ex - 16
            return (T["w1s_t"].ap()[:, fh * 512:(fh + 1) * 512].rearrange("(k p) f -> p k f", p=128),
                    T["w3s_t"].ap()[:, fh * 512:(fh + 1) * 512].rearrange("(k p) f -> p k f", p=128),
                    T["w2s_t"].ap()[fh * 512:(fh + 1) * 512, :].rearrange("(k p) m -> p k m", p=128),
                    T["comb_s"].ap()[ex:ex + 1, :])
        _moe_experts(env, S, A, dep, T, 18, srcfn, "comb_s", "hid_s", "w2b_s",
                     xnr, crow, gbsb, w1f, w1b, w3f, w3b, w2f, w2b, s1b, s2b, hbuf,
                     psG, psH, onesb1, [("Scb",), ("Sor", 1)])
        with nc.Block() as blk:
            S.emit(blk)
    env.release(mk)


def st_ffn0b(env):
    nc = env.nc
    B = dict(env.cnt)
    T = env.t
    mk = env.mark()
    S = Sched(env, B)
    A = S.add
    acc = env.alloc("fb_acc", (128, 8, 2048), F32)
    hidl = env.alloc("fb_hid", (128, 2, 4, 2048), BF16)
    w2c = env.alloc("fb_w2", (128, 2, 4, 1024), BF16)
    hid_r = T["hid_s"].ap().rearrange("(k p) t -> p k t", p=128)
    w2_r = T["w2b_s"].ap().rearrange("(k p) m -> p k m", p=128)
    ff_r = T["ffn0_s"].ap().rearrange("(m p) t -> p m t", p=128)

    def dep(*keys):
        out = []
        for k in keys:
            if k is None:
                continue
            if k in S.keyinfo:
                out.append(k)
            else:
                assert any(isinstance(x, int) and x < 0 for x in k), f"missing dep {k}"
        return out

    with (
        nc.psum_tensor("fb_ps0", [128, 512], F32) as ps0,
        nc.psum_tensor("fb_ps1", [128, 512], F32) as ps1,
    ):
        ps = [ps0, ps1]
        A("vector", ("az",), [], lambda e: e.memset(acc[:, :, :], 0.0))
        for ex in range(18):
            sl = ex % 2
            A("sync", ("Lh", ex), dep(("mm", ex - 2, 7, 3)),
              lambda e, sl=sl, ex=ex: e.dma_start(out=hidl[:, sl, :, :],
                                                  in_=hid_r[:, ex * 4:(ex + 1) * 4, :]), dma_sem=f"bh{sl}")
            A("sync", ("Lw", ex), dep(("mm", ex - 2, 7, 3)),
              lambda e, sl=sl, ex=ex: e.dma_start(out=w2c[:, sl, :, :],
                                                  in_=w2_r[:, ex * 4:(ex + 1) * 4, :]), dma_sem=f"bw{sl}")
            for m in range(8):
                for tt in range(4):
                    g = (ex * 32 + m * 4 + tt)
                    def emit_mm(e, sl=sl, m=m, tt=tt, g=g):
                        for k in range(4):
                            mm = e.matmul(ps[g % 2][:, :], w2c[:, sl, k, m * 128:(m + 1) * 128],
                                          hidl[:, sl, k, tt * 512:(tt + 1) * 512],
                                          start=(k == 0), stop=(k == 3))
                        return mm
                    pm, ptt = divmod(m * 4 + tt - 2, 4)
                    war = ("av", ex, pm, ptt) if m * 4 + tt >= 2 else ("av", ex - 1, 7, 2 + m * 4 + tt)
                    A("tensor", ("mm", ex, m, tt), dep(("Lh", ex), ("Lw", ex), war), emit_mm)
                    A("vector", ("av", ex, m, tt), dep(("mm", ex, m, tt), ("az",)),
                      lambda e, m=m, tt=tt, g=g: e.tensor_tensor(
                          acc[:, m, tt * 512:(tt + 1) * 512], ps[g % 2][:, :],
                          acc[:, m, tt * 512:(tt + 1) * 512], ALU.add))
        for m in range(8):
            A("gpsimd", ("Sf", m), dep(("av", 17, 7, 3)),
              lambda e, m=m: e.dma_start(out=ff_r[:, m, :], in_=acc[:, m, :]), dma_sem="bo")
        with nc.Block() as blk:
            S.emit(blk)
    env.release(mk)


STAGES["ffn0a"] = (["xn_s", "xn32_s", "wg0_t", "w1e_t", "w3e_t", "w2e_t", "w1s_t", "w3s_t", "w2s_t",
                    "ident_b", "ident_f"], ["hid_s", "w2b_s", "comb_s"], st_ffn0a)
STAGES["ffn0b"] = (["hid_s", "w2b_s"], ["ffn0_s"], st_ffn0b)


def _mkdep(S):
    def dep(*keys):
        out = []
        for k in keys:
            if k is None:
                continue
            if k in S.keyinfo:
                out.append(k)
            else:
                assert any(isinstance(x, int) and x < 0 for x in k), f"missing dep {k}"
        return out
    return dep


def st_res1(env):
    nc = env.nc
    B = dict(env.cnt)
    T = env.t
    mk = env.mark()
    S = Sched(env, B)
    A = S.add
    dep = _mkdep(S)
    xt = env.alloc("r1_x", (128, 2, 8, 512), F32)
    at = env.alloc("r1_a", (128, 2, 8, 512), F32)
    ft = env.alloc("r1_f", (128, 2, 8, 512), F32)
    xb = env.alloc("r1_xb", (128, 8, 512), BF16)
    x2 = env.alloc("r1_x2", (128, 8, 512), BF16)
    onesb = env.alloc("r1_ob", (128, 1), BF16)
    onesf = env.alloc("r1_of", (1, 128), F32)
    st0 = env.alloc("r1_st0", (1, 512), F32)
    ms = env.alloc("r1_ms", (1, 2, 512), F32)
    xo = env.alloc("r1_out", (128, 8, 512), BF16)
    xo32 = env.alloc("r1_out32", (128, 8, 512), F32)
    x_r = T["x_fm"].ap().rearrange("(c p) t -> p c t", p=128)
    a_r = T["attn0_s"].ap().rearrange("(c p) t -> p c t", p=128)
    f_r = T["ffn0_s"].ap().rearrange("(c p) t -> p c t", p=128)
    x1_r = T["x1_s"].ap().rearrange("(c p) t -> p c t", p=128)
    n1_r = T["xn1_s"].ap().rearrange("(c p) t -> p c t", p=128)
    n3_r = T["xn132_s"].ap().rearrange("(c p) t -> p c t", p=128)
    with (
        nc.psum_tensor("r1_st", [1, 2, 512], F32) as ps_st,
        nc.psum_tensor("r1_b", [128, 2, 512], F32) as ps_b,
    ):
        A("vector", ("ob",), [], lambda e: e.memset(onesb[:, :], 1.0))
        A("vector", ("of",), [], lambda e: e.memset(onesf[:, :], 1.0))
        for i in range(4):
            sl = i % 2
            A("sync", ("Lx", i), dep(("ap3", i - 2)),
              lambda e, sl=sl, i=i: e.dma_start(out=xt[:, sl, :, :], in_=x_r[:, :, RB + i * 512:RB + (i + 1) * 512]),
              dma_sem=f"rx{sl}")
            A("sync", ("La", i), dep(("ad1", i - 2)),
              lambda e, sl=sl, i=i: e.dma_start(out=at[:, sl, :, :], in_=a_r[:, :, i * 512:(i + 1) * 512]),
              dma_sem=f"ra{sl}")
            A("sync", ("Lf", i), dep(("ad2", i - 2)),
              lambda e, sl=sl, i=i: e.dma_start(out=ft[:, sl, :, :], in_=f_r[:, :, i * 512:(i + 1) * 512]),
              dma_sem=f"rf{sl}")
            A("vector", ("ad1", i), dep(("Lx", i), ("La", i)),
              lambda e, sl=sl: e.tensor_tensor(xt[:, sl, :, :], xt[:, sl, :, :], at[:, sl, :, :], ALU.add))
            A("vector", ("ad2", i), dep(("ad1", i), ("Lf", i)),
              lambda e, sl=sl: e.tensor_tensor(xt[:, sl, :, :], xt[:, sl, :, :], ft[:, sl, :, :], ALU.add))
            A("gpsimd", ("Sx", i), dep(("ad2", i)),
              lambda e, sl=sl, i=i: e.dma_start(out=x1_r[:, :, i * 512:(i + 1) * 512], in_=xt[:, sl, :, :]),
              dma_sem=f"r1{sl}")
            A("vector", ("cb", i), dep(("ad2", i), ("p1", i - 1)),
              lambda e, sl=sl: e.tensor_copy(xb[:, :, :], xt[:, sl, :, :]))
            A("scalar", ("sq", i), dep(("ad2", i), ("p2", i - 1)),
              lambda e, sl=sl: e.square(x2[:, :, :], xt[:, sl, :, :]))
            def emit_p1(e, i=i):
                for k in range(8):
                    mm = e.matmul(ps_st[:, 0, :], onesb[:, :], xb[:, k, :], start=(k == 0), stop=(k == 7))
                return mm
            A("tensor", ("p1", i), dep(("cb", i), ("ob",), ("a2", i - 1)), emit_p1)
            def emit_p2(e, i=i):
                for k in range(8):
                    mm = e.matmul(ps_st[:, 1, :], onesb[:, :], x2[:, k, :], start=(k == 0), stop=(k == 7))
                return mm
            A("tensor", ("p2", i), dep(("sq", i), ("ob",), ("a4", i - 1)), emit_p2)
            A("scalar", ("a2", i), dep(("p1", i), ("p3", i - 1)),
              lambda e: e.mul(ms[:, 0, :], ps_st[:, 0, :], 1.0 / HID))
            A("scalar", ("a3", i), dep(("a2", i), ("d2", i - 1)),
              lambda e: e.square(ms[:, 1, :], ms[:, 0, :]))
            A("scalar", ("a4", i), dep(("p2", i), ("d2", i - 1)),
              lambda e: e.mul(st0[:, :], ps_st[:, 1, :], 1.0 / HID))
            A("vector", ("d2", i), dep(("a3", i), ("a4", i), ("a5", i - 1)),
              lambda e: e.tensor_tensor(ms[:, 1, :], st0[:, :], ms[:, 1, :], ALU.subtract))
            A("vector", ("d3", i), dep(("d2", i)),
              lambda e: e.tensor_scalar(ms[:, 1, :], ms[:, 1, :], 1e-5, None, ALU.add))
            A("scalar", ("a5", i), dep(("d3", i), ("d4", i - 1)),
              lambda e: e.sqrt(st0[:, :], ms[:, 1, :]))
            A("vector", ("d4", i), dep(("a5", i), ("p3", i - 1)),
              lambda e: e.reciprocal(ms[:, 1, :], st0[:, :]))
            A("tensor", ("p3", i), dep(("a2", i), ("d4", i), ("of",), ("ap3", i - 1)),
              lambda e: (e.matmul(ps_b[:, 0, :], onesf[:, :], ms[:, 0, :], start=True, stop=True),
                         e.matmul(ps_b[:, 1, :], onesf[:, :], ms[:, 1, :], start=True, stop=True))[1])
            A("vector", ("ap1", i), dep(("p3", i), ("Sx", i)),
              lambda e, sl=sl: e.tensor_tensor(
                  xt[:, sl, :, :], xt[:, sl, :, :],
                  bass.AP(ps_b, 0, [[1024, 128], [0, 8], [1, 512]]), ALU.subtract))
            A("vector", ("ap2", i), dep(("ap1", i), ("S1", i - 1)),
              lambda e, sl=sl: e.tensor_tensor(
                  xo[:, :, :], xt[:, sl, :, :],
                  bass.AP(ps_b, 512, [[1024, 128], [0, 8], [1, 512]]), ALU.mult))
            A("vector", ("ap3", i), dep(("ap2", i), ("S2", i - 1)),
              lambda e, sl=sl: e.tensor_tensor(
                  xo32[:, :, :], xt[:, sl, :, :],
                  bass.AP(ps_b, 512, [[1024, 128], [0, 8], [1, 512]]), ALU.mult))
            A("gpsimd", ("S1", i), dep(("ap2", i)),
              lambda e, i=i: e.dma_start(out=n1_r[:, :, i * 512:(i + 1) * 512], in_=xo[:, :, :]),
              dma_sem=f"rn{i % 2}")
            A("gpsimd", ("S2", i), dep(("ap3", i)),
              lambda e, i=i: e.dma_start(out=n3_r[:, :, i * 512:(i + 1) * 512], in_=xo32[:, :, :]),
              dma_sem=f"rm{i % 2}")
        with nc.Block() as blk:
            S.emit(blk)
    env.release(mk)


def st_l1proj(env):
    nc = env.nc
    B = dict(env.cnt)
    T = env.t
    mk = env.mark()
    S = Sched(env, B)
    A = S.add
    dep = _mkdep(S)
    xn1 = env.alloc("lp_xn", (128, 8, 2048), BF16)
    wf = env.alloc("lp_wf", (128, 2, 8, 128), F32)
    wb = env.alloc("lp_wb", (128, 2, 8, 128), BF16)
    hb = env.alloc("lp_hb", (128, 2, 512), BF16)
    sqb = env.alloc("lp_sq", (128, 2, 512), BF16)
    rs = env.alloc("lp_rs", (128, 2, 2), F32)
    stat = env.alloc("lp_stat", (128, 48), F32)
    xn_r = T["xn1_s"].ap().rearrange("(k p) t -> p k t", p=128)
    wsrc = {0: "wq_t", 1: "wk_t", 2: "wv_t"}
    hdst = {0: "hq_s", 1: "hk_s", 2: "hv_s"}
    with (
        nc.psum_tensor("lp_ps0", [128, 512], F32) as ps0,
        nc.psum_tensor("lp_ps1", [128, 512], F32) as ps1,
    ):
        ps = [ps0, ps1]
        A("sync", ("Lx",), [], lambda e: e.dma_start(out=xn1[:, :, :], in_=xn_r), dma_sem="lA")
        A("vector", ("stz",), [], lambda e: e.memset(stat[:, :], 0.0))
        for p in range(3):
            w_r = T[wsrc[p]].ap().rearrange("(k p) m -> p k m", p=128)
            h_r = T[hdst[p]].ap().rearrange("(m p) t -> p m t", p=128)
            for m in range(8):
                u = p * 8 + m
                A("sync", ("Lw", p, m), dep(("cw", p, m - 2) if m >= 2 else ("cw", p - 1, 6 + m)),
                  lambda e, u=u, w_r=w_r, m=m: e.dma_start(out=wf[:, u % 2, :, :],
                                                           in_=w_r[:, :, m * 128:(m + 1) * 128]),
                  dma_sem=f"lw{u % 2}")
                A("vector", ("cw", p, m), dep(("Lw", p, m), ("mm", p, m - 2, 3) if m >= 2 else ("mm", p - 1, 6 + m, 3)),
                  lambda e, u=u: e.tensor_copy(wb[:, u % 2, :, :], wf[:, u % 2, :, :]))
                for tt in range(4):
                    g = u * 4 + tt
                    pg = g - 2
                    pwar = [("ev", pg // 4 // 8, (pg // 4) % 8, pg % 4)] if pg >= 0 else []
                    def emit_mm(e, u=u, m=m, tt=tt, g=g):
                        for k in range(8):
                            mm = e.matmul(ps[g % 2][:, :], wb[:, u % 2, k, :],
                                          xn1[:, k, tt * 512:(tt + 1) * 512], start=(k == 0), stop=(k == 7))
                        return mm
                    A("tensor", ("mm", p, m, tt), dep(("cw", p, m), ("Lx",), *pwar), emit_mm)
                    hwar = []
                    if pg >= 0:
                        pp_, pm_, pt_ = pg // 4 // 8, (pg // 4) % 8, pg % 4
                        hwar = [("Sh", pp_, pm_, pt_), ("sq", pp_, pm_, pt_), ("r1", pp_, pm_, pt_)]
                    A("vector", ("ev", p, m, tt), dep(("mm", p, m, tt), *hwar),
                      lambda e, g=g: e.tensor_copy(hb[:, g % 2, :], ps[g % 2][:, :]))
                    A("scalar", ("sq", p, m, tt), dep(("ev", p, m, tt), ("r2", p, m, tt - 2) if tt >= 2 else
                                                     (("r2", p, m - 1, 2 + tt) if m >= 1 else
                                                      (("r2", p - 1, 7, 2 + tt) if p >= 1 else None))),
                      lambda e, g=g: e.square(sqb[:, g % 2, :], hb[:, g % 2, :]))
                    A("vector", ("r1", p, m, tt), dep(("ev", p, m, tt), ("ac1", p, m, tt - 2) if tt >= 2 else
                                                     (("ac1", p, m - 1, 2 + tt) if m >= 1 else
                                                      (("ac1", p - 1, 7, 2 + tt) if p >= 1 else None))),
                      lambda e, g=g: e.tensor_reduce(rs[:, g % 2, 0:1], hb[:, g % 2, :], AX.X, ALU.add))
                    A("vector", ("r2", p, m, tt), dep(("sq", p, m, tt), ("ac2", p, m, tt - 2) if tt >= 2 else
                                                     (("ac2", p, m - 1, 2 + tt) if m >= 1 else
                                                      (("ac2", p - 1, 7, 2 + tt) if p >= 1 else None))),
                      lambda e, g=g: e.tensor_reduce(rs[:, g % 2, 1:2], sqb[:, g % 2, :], AX.X, ALU.add))
                    c0 = p * 16 + m * 2
                    A("vector", ("ac1", p, m, tt), dep(("r1", p, m, tt), ("stz",), ("ac1", p, m, tt - 1) if tt >= 1 else None),
                      lambda e, g=g, c0=c0: e.tensor_tensor(stat[:, c0:c0 + 1], rs[:, g % 2, 0:1],
                                                            stat[:, c0:c0 + 1], ALU.add))
                    A("vector", ("ac2", p, m, tt), dep(("r2", p, m, tt), ("stz",), ("ac2", p, m, tt - 1) if tt >= 1 else None),
                      lambda e, g=g, c0=c0: e.tensor_tensor(stat[:, c0 + 1:c0 + 2], rs[:, g % 2, 1:2],
                                                            stat[:, c0 + 1:c0 + 2], ALU.add))
                    A("gpsimd", ("Sh", p, m, tt), dep(("ev", p, m, tt)),
                      lambda e, g=g, h_r=h_r, m=m, tt=tt: e.dma_start(
                          out=h_r[:, m, tt * 512:(tt + 1) * 512], in_=hb[:, g % 2, :]),
                      dma_sem=f"lh{g % 2}")
        A("gpsimd", ("Sb",), dep(("ac1", 2, 7, 3), ("ac2", 2, 7, 3)),
          lambda e: e.dma_start(out=T["bn_loc"].ap(), in_=stat[:, :]), dma_sem="lB")
        with nc.Block() as blk:
            S.emit(blk)
    env.release(mk)


def st_l1cc1(env):
    nc = env.nc
    B = dict(env.cnt)
    T = env.t
    cb_in = nc.dram_tensor("lc1_in", [128, 48], F32, kind="Internal")
    cb_out = nc.dram_tensor("lc1_out", [128 * NCORES, 48], F32, kind="Internal")
    with nc.Block() as blk:
        @blk.gpsimd
        def _(e):
            env.waits(e, B)
            env.inc("qp", e.dma_start(out=cb_in.ap(), in_=T["bn_loc"].ap()), 16)
            e.wait_ge(env.sem["qp"], env.cnt["qp"])
            env.inc("cc2", e.collective_compute(
                "AllGather", ALU.bypass, replica_groups=[list(range(NCORES))],
                ins=[cb_in.ap()], outs=[cb_out.ap()]), 1)
            e.wait_ge(env.sem["cc2"], env.cnt["cc2"])
            env.inc("qp", e.dma_start(out=T["bn_all"].ap(), in_=cb_out.ap()), 16)
            e.wait_ge(env.sem["qp"], env.cnt["qp"])


def st_l1spike(env):
    nc = env.nc
    B = dict(env.cnt)
    T = env.t
    mk = env.mark()
    S = Sched(env, B)
    A = S.add
    dep = _mkdep(S)
    bna = env.alloc("ls_bna", (128, 8, 48), F32)
    bns = env.alloc("ls_bns", (128, 48), F32)
    bnp = env.alloc("ls_bnp", (128, 48), F32)
    mnb = env.alloc("ls_mn", (128, 3, 8), F32)
    qb_ = env.alloc("ls_qv", (128, 3, 8), F32)
    m2b = env.alloc("ls_m2", (128, 3, 8), F32)
    a4 = env.alloc("ls_a4", (128, 3, 8), F32)
    b4 = env.alloc("ls_b4", (128, 3, 8), F32)
    identb = env.alloc("ls_idb", (128, 128), BF16)
    hq = env.alloc("ls_hq", (128, 2, 512), BF16)
    hk = env.alloc("ls_hk", (128, 2, 512), BF16)
    hv = env.alloc("ls_hv", (128, 2, 512), BF16)
    tf = env.alloc("ls_tf", (128, 2, 512), F32)
    qsp = env.alloc("ls_qs", (128, 2, 512), BF16)
    ksp = env.alloc("ls_ks", (128, 2, 512), BF16)
    vsp = env.alloc("ls_vs", (128, 2, 512), BF16)
    qkb = env.alloc("ls_qk", (128, 2, 512), BF16)
    vtok = env.alloc("ls_vt", (128, 2, 4, 128), BF16)
    vvb = env.alloc("ls_vv", (128, 512), F32)
    bn_r = T["bn_all"].ap().rearrange("(c p) k -> p c k", p=128)
    qk_r = T["qk_s"].ap().rearrange("(m p) t -> p m t", p=128)
    hsrc = {0: "hq_s", 1: "hk_s", 2: "hv_s"}
    with (
        nc.psum_tensor("ls_pV0", [128, 512], F32) as psV0,
        nc.psum_tensor("ls_pV1", [128, 512], F32) as psV1,
    ):
        psVs = [psV0, psV1]
        A("sync", ("Lb",), [], lambda e: e.dma_start(out=bna[:, :, :], in_=bn_r), dma_sem="sA1")
        A("sync", ("Lp",), [], lambda e: e.dma_start(out=bnp[:, :], in_=T["bnp_sb"].ap()), dma_sem="sA2")
        A("sync", ("Li",), [], lambda e: e.dma_start(out=identb[:, :], in_=T["ident_b"].ap()), dma_sem="sA3")
        A("vector", ("su", 0), dep(("Lb",)), lambda e: e.tensor_copy(bns[:, :], bna[:, 0, :]))
        for j in range(1, 8):
            A("vector", ("su", j), dep(("su", j - 1)),
              lambda e, j=j: e.tensor_tensor(bns[:, :], bns[:, :], bna[:, j, :], ALU.add))
        NTOK = float(NCORES * TLOC)
        for p in range(3):
            c0 = p * 16
            A("vector", ("mn", p), dep(("su", 7)),
              lambda e, p=p, c0=c0: e.tensor_scalar(
                  mnb[:, p, :], bass.AP(bns, c0, [[48, 128], [2, 8]]), 1.0 / NTOK, None, ALU.mult))
            A("vector", ("q1", p), dep(("su", 7)),
              lambda e, p=p, c0=c0: e.tensor_scalar(
                  qb_[:, p, :], bass.AP(bns, c0 + 1, [[48, 128], [2, 8]]), 1.0 / NTOK, None, ALU.mult))
            A("scalar", ("m2", p), dep(("mn", p)),
              lambda e, p=p: e.square(m2b[:, p, :], mnb[:, p, :]))
            A("vector", ("vr", p), dep(("m2", p), ("q1", p)),
              lambda e, p=p: e.tensor_tensor(qb_[:, p, :], qb_[:, p, :], m2b[:, p, :], ALU.subtract))
            A("vector", ("ve", p), dep(("vr", p)),
              lambda e, p=p: e.tensor_scalar(qb_[:, p, :], qb_[:, p, :], 1e-5, None, ALU.add))
            A("scalar", ("sr", p), dep(("ve", p)),
              lambda e, p=p: e.sqrt(qb_[:, p, :], qb_[:, p, :]))
            A("vector", ("rc", p), dep(("sr", p)),
              lambda e, p=p: e.reciprocal(qb_[:, p, :], qb_[:, p, :]))
            A("vector", ("aa", p), dep(("rc", p), ("Lp",)),
              lambda e, p=p: e.tensor_tensor(a4[:, p, :], qb_[:, p, :],
                                             bass.AP(bnp, 16 * p, [[48, 128], [1, 8]]), ALU.mult))
            A("vector", ("bb", p), dep(("aa", p)),
              lambda e, p=p: e.tensor_tensor(b4[:, p, :], mnb[:, p, :], a4[:, p, :], ALU.mult))
            A("vector", ("bc", p), dep(("bb", p)),
              lambda e, p=p: e.tensor_tensor(b4[:, p, :], bass.AP(bnp, 16 * p + 8, [[48, 128], [1, 8]]),
                                             b4[:, p, :], ALU.subtract))
            A("vector", ("a4x", p), dep(("bc", p)),
              lambda e, p=p: e.tensor_scalar(a4[:, p, :], a4[:, p, :], 4.0, None, ALU.mult))
            A("vector", ("b4x", p), dep(("a4x", p)),
              lambda e, p=p: e.tensor_scalar(b4[:, p, :], b4[:, p, :], 4.0, -4.0, ALU.mult, ALU.add))
        sbufs = {0: (hq, qsp), 1: (hk, ksp), 2: (hv, vsp)}
        for m in range(8):
            for tt in range(4):
                g = m * 4 + tt
                sl = g % 2
                for p in range(3):
                    hb_, sp_ = sbufs[p]
                    h_r = T[hsrc[p]].ap().rearrange("(m p) t -> p m t", p=128)
                    A("sync", ("Lh", p, m, tt), dep(("af", p, m, tt - 2) if g >= 2 else None,
                                                    ("af", p, m - 1, 2 + tt) if g >= 2 and tt < 2 else None),
                      lambda e, sl=sl, hb_=hb_, h_r=h_r, m=m, tt=tt: e.dma_start(
                          out=hb_[:, sl, :], in_=h_r[:, m, tt * 512:(tt + 1) * 512]),
                      dma_sem=f"s{p}{sl}")
                    pafg = g - 2 if p == 0 else g
                    pafp = 2 if p == 0 else p - 1
                    afwar = ("sg", pafp, pafg // 4, pafg % 4) if pafg >= 0 else None
                    A("vector", ("af", p, m, tt), dep(("Lh", p, m, tt), ("b4x", 2), afwar),
                      lambda e, sl=sl, hb_=hb_, p=p, m=m: e.tensor_scalar(
                          tf[:, sl, :], hb_[:, sl, :], a4[:, p, m:m + 1], b4[:, p, m:m + 1],
                          ALU.mult, ALU.add))
                    A("scalar", ("sg", p, m, tt), dep(("af", p, m, tt),
                                                      ("qk", m, tt - 2) if p == 0 and g >= 2 else None,
                                                      ("qk", m - 1, 2 + tt) if p == 0 and g >= 2 and tt < 2 else None,
                                                      ("Tv", m, tt - 2) if p == 2 and g >= 2 else None,
                                                      ("Tv", m - 1, 2 + tt) if p == 2 and g >= 2 and tt < 2 else None),
                      lambda e, sl=sl, sp_=sp_: e.activation(sp_[:, sl, :], tf[:, sl, :], AF.Sigmoid))
                A("vector", ("qk", m, tt), dep(("sg", 0, m, tt), ("sg", 1, m, tt),
                                               ("Sq", m, tt - 2) if g >= 2 else None,
                                               ("Sq", m - 1, 2 + tt) if g >= 2 and tt < 2 else None),
                  lambda e, sl=sl: e.tensor_tensor(qkb[:, sl, :], qsp[:, sl, :], ksp[:, sl, :], ALU.mult))
                A("gpsimd", ("Sq", m, tt), dep(("qk", m, tt)),
                  lambda e, sl=sl, m=m, tt=tt: e.dma_start(out=qk_r[:, m, tt * 512:(tt + 1) * 512],
                                                           in_=qkb[:, sl, :]), dma_sem=f"sq{sl}")
                pg2 = g - 2
                A("sync", ("Tv", m, tt), dep(("sg", 2, m, tt),
                                             ("vv", pg2 // 4, pg2 % 4, 3) if pg2 >= 0 else None),
                  lambda e, sl=sl: e.dma_start_transpose(
                      bass.AP(vtok, sl * 512, [[1024, 128], [128, 4], [1, 128]]), vsp[:, sl, :]),
                  dma_sem=f"tv{sl}")
                for ts in range(4):
                    def emit_vv(e, m=m, tt=tt, ts=ts, sl=sl):
                        for hh in range(2):
                            mm = e.matmul(psVs[hh][hh * 64:hh * 64 + 64, m * 64:(m + 1) * 64],
                                          vtok[:, sl, ts, hh * 64:(hh + 1) * 64],
                                          vtok[:, sl, ts, hh * 64:(hh + 1) * 64],
                                          start=(tt == 0 and ts == 0), stop=(tt == 3 and ts == 3))
                        return mm
                    A("tensor", ("vv", m, tt, ts), dep(("Tv", m, tt)), emit_vv,
                      drain=(tt == 3 and ts == 3))
        A("vector", ("vve", 0), dep(("vv", 7, 3, 3)),
          lambda e: e.tensor_copy(vvb[0:64, :], psV0[0:64, :]))
        A("vector", ("vve", 1), dep(("vv", 7, 3, 3)),
          lambda e: e.tensor_copy(vvb[64:128, :], psV1[64:128, :]))
        A("gpsimd", ("Sv",), dep(("vve", 0), ("vve", 1)),
          lambda e: e.dma_start(out=T["vv_loc"].ap(), in_=vvb[:, :]), dma_sem="sV")
        with nc.Block() as blk:
            S.emit(blk)
    env.release(mk)


def st_l1cc2(env):
    nc = env.nc
    B = dict(env.cnt)
    T = env.t
    cb_in = nc.dram_tensor("lc2_in", [128, 512], F32, kind="Internal")
    cb_out = nc.dram_tensor("lc2_out", [128 * NCORES, 512], F32, kind="Internal")
    with nc.Block() as blk:
        @blk.gpsimd
        def _(e):
            env.waits(e, B)
            env.inc("qp", e.dma_start(out=cb_in.ap(), in_=T["vv_loc"].ap()), 16)
            e.wait_ge(env.sem["qp"], env.cnt["qp"])
            env.inc("cc3", e.collective_compute(
                "AllGather", ALU.bypass, replica_groups=[list(range(NCORES))],
                ins=[cb_in.ap()], outs=[cb_out.ap()]), 1)
            e.wait_ge(env.sem["cc3"], env.cnt["cc3"])
            env.inc("qp", e.dma_start(out=T["vv_all"].ap(), in_=cb_out.ap()), 16)
            e.wait_ge(env.sem["qp"], env.cnt["qp"])


def st_l1out(env):
    nc = env.nc
    B = dict(env.cnt)
    T = env.t
    mk = env.mark()
    S = Sched(env, B)
    A = S.add
    dep = _mkdep(S)
    vva = env.alloc("lo_vva", (128, 8, 512), F32)
    vvs = env.alloc("lo_vvs", (128, 512), F32)
    vvb = env.alloc("lo_vvb", (128, 512), BF16)
    qkt = env.alloc("lo_qk", (128, 2, 512), BF16)
    obuf = env.alloc("lo_ob", (128, 8, 2048), BF16)
    wf = env.alloc("lo_wf", (128, 2, 8, 128), F32)
    wb = env.alloc("lo_wb", (128, 2, 8, 128), BF16)
    bpt = env.alloc("lo_bp", (128, 8), F32)
    atb = env.alloc("lo_at", (128, 2, 512), F32)
    vv_r = T["vv_all"].ap().rearrange("(c p) k -> p c k", p=128)
    qk_r = T["qk_s"].ap().rearrange("(m p) t -> p m t", p=128)
    wp_r = T["wp_t"].ap().rearrange("(k p) m -> p k m", p=128)
    at_r = T["attn1_s"].ap().rearrange("(m p) t -> p m t", p=128)
    with (
        nc.psum_tensor("lo_ps0", [128, 512], F32) as ps0,
        nc.psum_tensor("lo_ps1", [128, 512], F32) as ps1,
    ):
        ps = [ps0, ps1]
        A("sync", ("Lv",), [], lambda e: e.dma_start(out=vva[:, :, :], in_=vv_r), dma_sem="oA1")
        A("sync", ("Lb",), [], lambda e: e.dma_start(out=bpt[:, :], in_=T["bp_sb"].ap()), dma_sem="oA2")
        A("vector", ("su", 0), dep(("Lv",)), lambda e: e.tensor_copy(vvs[:, :], vva[:, 0, :]))
        for j in range(1, 8):
            A("vector", ("su", j), dep(("su", j - 1)),
              lambda e, j=j: e.tensor_tensor(vvs[:, :], vvs[:, :], vva[:, j, :], ALU.add))
        A("vector", ("sc",), dep(("su", 7)),
          lambda e: e.tensor_scalar(vvb[:, :], vvs[:, :], 0.125, None, ALU.mult))
        for m in range(8):
            for tt in range(4):
                g = m * 4 + tt
                A("sync", ("Lq", m, tt), dep(("om", (g - 2) // 4, (g - 2) % 4) if g >= 2 else None),
                  lambda e, g=g, m=m, tt=tt: e.dma_start(out=qkt[:, g % 2, :],
                                                         in_=qk_r[:, m, tt * 512:(tt + 1) * 512]),
                  dma_sem=f"oq{g % 2}")
                def emit_om(e, m=m, tt=tt, g=g):
                    for hh in range(2):
                        mm = e.matmul(ps[g % 2][hh * 64:hh * 64 + 64, :],
                                      vvb[hh * 64:hh * 64 + 64, m * 64:(m + 1) * 64],
                                      qkt[hh * 64:hh * 64 + 64, g % 2, :], start=True, stop=True)
                    return mm
                A("tensor", ("om", m, tt), dep(("Lq", m, tt), ("sc",),
                                               ("oe", (g - 2) // 4, (g - 2) % 4) if g >= 2 else None), emit_om)
                A("vector", ("oe", m, tt), dep(("om", m, tt)),
                  lambda e, g=g, m=m, tt=tt: e.tensor_copy(obuf[:, m, tt * 512:(tt + 1) * 512],
                                                           ps[g % 2][:, :]))
        for m in range(8):
            A("sync", ("Lw", m), dep(("cw", m - 2)),
              lambda e, m=m: e.dma_start(out=wf[:, m % 2, :, :], in_=wp_r[:, :, m * 128:(m + 1) * 128]),
              dma_sem=f"ow{m % 2}")
            A("vector", ("cw", m), dep(("Lw", m), ("pm", m - 2, 3)),
              lambda e, m=m: e.tensor_copy(wb[:, m % 2, :, :], wf[:, m % 2, :, :]))
            for tt in range(4):
                g = m * 4 + tt
                def emit_pm(e, m=m, tt=tt, g=g):
                    for k in range(8):
                        mm = e.matmul(ps[g % 2][:, :], wb[:, m % 2, k, :],
                                      obuf[:, k, tt * 512:(tt + 1) * 512], start=(k == 0), stop=(k == 7))
                    return mm
                A("tensor", ("pm", m, tt), dep(("cw", m), ("oe", 7, 3),
                                               ("pe2", (g - 2) // 4, (g - 2) % 4) if g >= 2 else ("oe", 7, 3)), emit_pm)
                A("vector", ("pe2", m, tt), dep(("pm", m, tt), ("Lb",),
                                                ("Sa", (g - 2) // 4, (g - 2) % 4) if g >= 2 else None),
                  lambda e, g=g, m=m: e.tensor_scalar(atb[:, g % 2, :], ps[g % 2][:, :],
                                                      bpt[:, m:m + 1], None, ALU.add))
                A("gpsimd", ("Sa", m, tt), dep(("pe2", m, tt)),
                  lambda e, g=g, m=m, tt=tt: e.dma_start(out=at_r[:, m, tt * 512:(tt + 1) * 512],
                                                         in_=atb[:, g % 2, :]), dma_sem=f"oa{g % 2}")
        with nc.Block() as blk:
            S.emit(blk)
    env.release(mk)


def st_fin(env):
    nc = env.nc
    B = dict(env.cnt)
    T = env.t
    mk = env.mark()
    S = Sched(env, B)
    A = S.add
    dep = _mkdep(S)
    xt = env.alloc("fn_x", (128, 2, 8, 512), F32)
    at = env.alloc("fn_a", (128, 2, 8, 512), F32)
    ft = env.alloc("fn_f", (128, 2, 8, 512), F32)
    x_r = T["x1_s"].ap().rearrange("(c p) t -> p c t", p=128)
    a_r = T["attn1_s"].ap().rearrange("(c p) t -> p c t", p=128)
    f_r = T["ffn1_s"].ap().rearrange("(c p) t -> p c t", p=128)
    y_r = T["y_out"].ap().rearrange("(c p) t -> p c t", p=128)
    for i in range(4):
        sl = i % 2
        A("sync", ("Lx", i), dep(("Sy", i - 2)),
          lambda e, sl=sl, i=i: e.dma_start(out=xt[:, sl, :, :], in_=x_r[:, :, i * 512:(i + 1) * 512]),
          dma_sem=f"yx{sl}")
        A("sync", ("La", i), dep(("ad1", i - 2)),
          lambda e, sl=sl, i=i: e.dma_start(out=at[:, sl, :, :], in_=a_r[:, :, i * 512:(i + 1) * 512]),
          dma_sem=f"ya{sl}")
        A("sync", ("Lf", i), dep(("ad2", i - 2)),
          lambda e, sl=sl, i=i: e.dma_start(out=ft[:, sl, :, :], in_=f_r[:, :, i * 512:(i + 1) * 512]),
          dma_sem=f"yf{sl}")
        A("vector", ("ad1", i), dep(("Lx", i), ("La", i), ("Sy", i - 2)),
          lambda e, sl=sl: e.tensor_tensor(xt[:, sl, :, :], xt[:, sl, :, :], at[:, sl, :, :], ALU.add))
        A("vector", ("ad2", i), dep(("ad1", i), ("Lf", i)),
          lambda e, sl=sl: e.tensor_tensor(xt[:, sl, :, :], xt[:, sl, :, :], ft[:, sl, :, :], ALU.add))
        A("gpsimd", ("Sy", i), dep(("ad2", i)),
          lambda e, sl=sl, i=i: e.dma_start(out=y_r[:, :, i * 512:(i + 1) * 512], in_=xt[:, sl, :, :]),
          dma_sem=f"yo{sl}")
    with nc.Block() as blk:
        S.emit(blk)
    env.release(mk)


STAGES["res1"] = (["x_fm", "attn0_s", "ffn0_s"], ["x1_s", "xn1_s", "xn132_s"], st_res1)
STAGES["l1proj"] = (["xn1_s", "wq_t", "wk_t", "wv_t"], ["hq_s", "hk_s", "hv_s", "bn_loc"], st_l1proj)
STAGES["l1cc1"] = (["bn_loc"], ["bn_all"], st_l1cc1)
STAGES["l1spike"] = (["hq_s", "hk_s", "hv_s", "bn_all", "bnp_sb", "ident_b"],
                     ["qk_s", "vv_loc"], st_l1spike)
STAGES["l1cc2"] = (["vv_loc"], ["vv_all"], st_l1cc2)
STAGES["l1out"] = (["vv_all", "qk_s", "wp_t", "bp_sb"], ["attn1_s"], st_l1out)
STAGES["fin"] = (["x1_s", "attn1_s", "ffn1_s"], ["y_out"], st_fin)


def st_ffn1a(env):
    nc = env.nc
    B = dict(env.cnt)
    T = env.t
    mk = env.mark()
    S = Sched(env, B)
    A = S.add
    dep = _mkdep(S)
    xnr = env.alloc("g1_xn", (128, 8, 2048), BF16)
    xg32 = env.alloc("g1_xg", (128, 8, 512), F32)
    wgf = env.alloc("g1_wgf", (128, 8, 8), F32)
    identb = env.alloc("g1_idb", (128, 128), BF16)
    identf = env.alloc("g1_idf", (128, 128), F32)
    onesb1 = env.alloc("g1_ob1", (1, 128), BF16)
    scf = env.alloc("g1_scf", (8, 2048), F32)
    schb = env.alloc("g1_schb", (16, 2048), BF16)
    sclb = env.alloc("g1_sclb", (16, 2048), BF16)
    scth = env.alloc("g1_scth", (128, 16, 16), BF16)
    sctl = env.alloc("g1_sctl", (128, 16, 16), BF16)
    sctf = env.alloc("g1_sct", (128, 16, 16), F32)
    g_m1 = env.alloc("g1_m1", (128, 16, 1), F32)
    g_m2 = env.alloc("g1_m2", (128, 16, 1), F32)
    g_ms = env.alloc("g1_ms", (128, 16, 8), F32)
    g_t1 = env.alloc("g1_t1", (128, 16, 1), F32)
    g_tf = env.alloc("g1_tf", (128, 16, 8), F32)
    g_sg = env.alloc("g1_sg", (128, 16, 8), F32)
    g_mk = env.alloc("g1_mk", (128, 16, 8), F32)
    comb = env.alloc("g1_comb", (128, 16, 8), BF16)
    combT = env.alloc("g1_combT", (128, 128), BF16)
    crow = env.alloc("g1_crow", (1, 2048), BF16)
    gbsb = env.alloc("g1_gbsb", (128, 2048), BF16)
    w1f = env.alloc("g1_w1f", (128, 2, 8, 512), F32)
    w1b = env.alloc("g1_w1b", (128, 2, 8, 512), BF16)
    w3f = env.alloc("g1_w3f", (128, 8, 512), F32)
    w3b = env.alloc("g1_w3b", (128, 2, 8, 512), BF16)
    w2f = env.alloc("g1_w2f", (128, 4, 1024), F32)
    w2b = env.alloc("g1_w2b", (128, 4, 1024), BF16)
    s1b = env.alloc("g1_s1", (128, 2, 512), BF16)
    s2b = env.alloc("g1_s2b", (128, 2, 512), BF16)
    hbuf = env.alloc("g1_hb", (128, 2, 512), BF16)

    xn_r = T["xn1_s"].ap().rearrange("(k p) t -> p k t", p=128)
    xn32_r = T["xn132_s"].ap().rearrange("(k p) t -> p k t", p=128)
    wg_r = T["wg1_t"].ap().rearrange("(k p) m -> p k m", p=128)

    def vw(t, off, *dims):
        ps = 1
        for d in t.shape[1:]:
            ps *= int(d)
        return bass.AP(t, off, [[ps, 128]] + [[int(a), int(b)] for a, b in dims])

    with (
        nc.psum_tensor("g1_psS", [8, 512], F32) as psS,
        nc.psum_tensor("g1_psG", [128, 512], F32) as psG,
        nc.psum_tensor("g1_psH0", [128, 512], F32) as psH0,
        nc.psum_tensor("g1_psH1", [128, 512], F32) as psH1,
    ):
        psH = [psH0, psH1]
        A("sync", ("Lx",), [], lambda e: e.dma_start(out=xnr[:, :, :], in_=xn_r), dma_sem="gA1")
        A("sync", ("Lg",), [], lambda e: e.dma_start(out=wgf[:, :, :], in_=wg_r), dma_sem="gA2")
        A("sync", ("Li1",), [], lambda e: e.dma_start(out=identb[:, :], in_=T["ident_b"].ap()), dma_sem="gA3")
        A("sync", ("Li2",), [], lambda e: e.dma_start(out=identf[:, :], in_=T["ident_f"].ap()), dma_sem="gA4")
        A("vector", ("ob1",), [], lambda e: e.memset(onesb1[:, :], 1.0))
        for tt in range(4):
            A("sync", ("Lx32", tt), dep(("sm", tt - 1)),
              lambda e, tt=tt: e.dma_start(out=xg32[:, :, :],
                                           in_=xn32_r[:, :, tt * 512:(tt + 1) * 512]),
              dma_sem=f"gx{tt % 2}")
            def emit_sc(e, tt=tt):
                for k in range(8):
                    mm = e.matmul(psS[:, :], wgf[:, k, :], xg32[:, k, :],
                                  start=(k == 0), stop=(k == 7))
                return mm
            A("tensor", ("sm", tt), dep(("Lg",), ("Lx32", tt), ("se", tt - 1)), emit_sc)
            A("scalar", ("se", tt), dep(("sm", tt)),
              lambda e, tt=tt: e.activation(scf[:, tt * 512:(tt + 1) * 512], psS[:, :], AF.Copy))
        A("vector", ("shz",), [], lambda e: e.memset(schb[:, :], 0.0))
        A("vector", ("shz2",), [], lambda e: e.memset(sclb[:, :], 0.0))
        A("vector", ("sh1",), dep(("se", 3), ("shz",)), lambda e: e.tensor_copy(schb[0:8, :], scf[:, :]))
        A("vector", ("sh2",), dep(("sh1",), ("shz2",)), lambda e: e.tensor_tensor(sclb[0:8, :], scf[:, :], schb[0:8, :], ALU.subtract))
        A("sync", ("Tsh",), dep(("sh1",)),
          lambda e: e.dma_start_transpose(bass.AP(scth, 0, [[256, 128], [16, 16], [1, 16]]), schb[:, :]),
          dma_sem="gT1")
        A("sync", ("Tsl",), dep(("sh2",)),
          lambda e: e.dma_start_transpose(bass.AP(sctl, 0, [[256, 128], [16, 16], [1, 16]]), sclb[:, :]),
          dma_sem="gT2")
        A("vector", ("te", 15), dep(("Tsh",), ("Tsl",)),
          lambda e: e.tensor_tensor(sctf[:, :, :], scth[:, :, :], sctl[:, :, :], ALU.add))
        sct = sctf[:, :, 0:8]
        L = ("te", 15)
        scv = lambda: bass.AP(sctf, 0, [[256, 128], [16, 16], [1, 8]])
        A("vector", ("w0",), dep(L), lambda e: e.tensor_reduce(g_m1[:, :, 0], scv(), AX.X, ALU.max))
        A("vector", ("w1",), dep(("w0",)),
          lambda e: e.tensor_tensor(g_mk[:, :, :], scv(), vw(g_m1, 0, (1, 16), (0, 8)), ALU.is_ge))
        A("vector", ("w2",), dep(("w1",)),
          lambda e: e.scalar_tensor_tensor(g_ms[:, :, :], g_mk[:, :, :], -BIG, scv(), ALU.mult, ALU.add))
        A("vector", ("w3",), dep(("w2",)), lambda e: e.tensor_reduce(g_m2[:, :, 0], g_ms[:, :, :], AX.X, ALU.max))
        A("vector", ("w4",), dep(("w3",)),
          lambda e: e.tensor_tensor(g_t1[:, :, :], g_m1[:, :, :], g_m2[:, :, :], ALU.add))
        A("vector", ("w5",), dep(("w4",)),
          lambda e: e.scalar_tensor_tensor(g_tf[:, :, :], scv(), 2.0,
                                           vw(g_t1, 0, (1, 16), (0, 8)), ALU.mult, ALU.subtract))
        A("scalar", ("w6",), dep(("w5",)), lambda e: e.activation(g_sg[:, :, :], g_tf[:, :, :], AF.Sigmoid))
        A("vector", ("w7",), dep(("w3",)),
          lambda e: e.tensor_tensor(g_mk[:, :, :], scv(), vw(g_m2, 0, (1, 16), (0, 8)), ALU.is_ge))
        A("vector", ("w8",), dep(("w6",), ("w7",)),
          lambda e: e.tensor_tensor(comb[:, :, :], g_mk[:, :, :], g_sg[:, :, :], ALU.mult))
        A("sync", ("Ttb2", 15), dep(("w8",)),
          lambda e: e.dma_start_transpose(
              bass.AP(combT, 0, [[128, 128], [1, 128]]),
              bass.AP(comb, 0, [[128, 128], [1, 128]])),
          dma_sem="gU0")
        A("gpsimd", ("Scb",), dep(("Ttb2", 15)),
          lambda e: e.dma_start(
              out=bass.AP(T["comb1_s"], 0, [[128, 16], [2048, 8], [1, 128]]),
              in_=combT[:, :]), dma_sem="gB1")
        def srcfn(ex):
            eidx, fq = divmod(ex, 4)
            return (T["l1w1_t"].ap()[eidx][:, fq * 512:(fq + 1) * 512].rearrange("(k p) f -> p k f", p=128),
                    T["l1w2_t"].ap()[eidx][:, fq * 512:(fq + 1) * 512].rearrange("(k p) f -> p k f", p=128),
                    T["l1w3_t"].ap()[eidx][fq * 512:(fq + 1) * 512, :].rearrange("(k p) m -> p k m", p=128),
                    T["comb1_s"].ap()[eidx:eidx + 1, :])
        _moe_experts(env, S, A, dep, T, 32, srcfn, "comb1_s", "hid1_s", "w3b1_s",
                     xnr, crow, gbsb, w1f, w1b, w3f, w3b, w2f, w2b, s1b, s2b, hbuf,
                     psG, psH, onesb1, [("Scb",)])
        with nc.Block() as blk:
            S.emit(blk)
    env.release(mk)


def st_ffn1b(env):
    nc = env.nc
    B = dict(env.cnt)
    T = env.t
    mk = env.mark()
    S = Sched(env, B)
    A = S.add
    dep = _mkdep(S)
    acc = env.alloc("gb_acc", (128, 8, 2048), F32)
    hidl = env.alloc("gb_hid", (128, 2, 4, 2048), BF16)
    w2c = env.alloc("gb_w2", (128, 2, 4, 1024), BF16)
    hid_r = T["hid1_s"].ap().rearrange("(k p) t -> p k t", p=128)
    w2_r = T["w3b1_s"].ap().rearrange("(k p) m -> p k m", p=128)
    ff_r = T["ffn1_s"].ap().rearrange("(m p) t -> p m t", p=128)
    with (
        nc.psum_tensor("gb_ps0", [128, 512], F32) as ps0,
        nc.psum_tensor("gb_ps1", [128, 512], F32) as ps1,
    ):
        ps = [ps0, ps1]
        A("vector", ("az",), [], lambda e: e.memset(acc[:, :, :], 0.0))
        for ex in range(32):
            sl = ex % 2
            A("sync", ("Lh", ex), dep(("mm", ex - 2, 7, 3)),
              lambda e, sl=sl, ex=ex: e.dma_start(out=hidl[:, sl, :, :],
                                                  in_=hid_r[:, ex * 4:(ex + 1) * 4, :]), dma_sem=f"bh{sl}")
            A("sync", ("Lw", ex), dep(("mm", ex - 2, 7, 3)),
              lambda e, sl=sl, ex=ex: e.dma_start(out=w2c[:, sl, :, :],
                                                  in_=w2_r[:, ex * 4:(ex + 1) * 4, :]), dma_sem=f"bw{sl}")
            for m in range(8):
                for tt in range(4):
                    g = (ex * 32 + m * 4 + tt)
                    def emit_mm(e, sl=sl, m=m, tt=tt, g=g):
                        for k in range(4):
                            mm = e.matmul(ps[g % 2][:, :], w2c[:, sl, k, m * 128:(m + 1) * 128],
                                          hidl[:, sl, k, tt * 512:(tt + 1) * 512],
                                          start=(k == 0), stop=(k == 3))
                        return mm
                    pm, ptt = divmod(m * 4 + tt - 2, 4)
                    war = ("av", ex, pm, ptt) if m * 4 + tt >= 2 else ("av", ex - 1, 7, 2 + m * 4 + tt)
                    A("tensor", ("mm", ex, m, tt), dep(("Lh", ex), ("Lw", ex), war), emit_mm)
                    A("vector", ("av", ex, m, tt), dep(("mm", ex, m, tt), ("az",)),
                      lambda e, m=m, tt=tt, g=g: e.tensor_tensor(
                          acc[:, m, tt * 512:(tt + 1) * 512], ps[g % 2][:, :],
                          acc[:, m, tt * 512:(tt + 1) * 512], ALU.add))
        for m in range(8):
            A("gpsimd", ("Sf", m), dep(("av", 31, 7, 3)),
              lambda e, m=m: e.dma_start(out=ff_r[:, m, :], in_=acc[:, m, :]), dma_sem="bo")
        with nc.Block() as blk:
            S.emit(blk)
    env.release(mk)


STAGES["ffn1a"] = (["xn1_s", "xn132_s", "wg1_t", "l1w1_t", "l1w2_t", "l1w3_t",
                    "ident_b", "ident_f"], ["hid1_s", "w3b1_s", "comb1_s"], st_ffn1a)
STAGES["ffn1b"] = (["hid1_s", "w3b1_s"], ["ffn1_s"], st_ffn1b)

GROUPS = [["ln1"], ["mam_proj"], ["mam_p1"], ["mam_cc"], ["mam_scan"], ["mam_p2"],
          ["ffn0a"], ["ffn0b"], ["res1"], ["l1proj"], ["l1cc1"], ["l1spike"],
          ["l1cc2"], ["l1out"], ["ffn1a"], ["ffn1b"], ["fin"]]



def build_group(stage_names, later_reads, export_all=False):
    nc = bass.Bass("TRN2", target_bir_lowering=False)
    env = Env(nc)
    reads, writes = set(), set()
    for sname in stage_names:
        r, w, _ = STAGES[sname]
        reads |= set(r)
        writes |= set(w)
    in_names, out_names = [], []
    for name in sorted(reads | writes):
        shape, npdt, bdt, cls = TENSORS[name]
        if name in reads and name not in writes:
            kind = "ExternalInput"
            in_names.append(name)
        elif cls == "out" or name in later_reads or export_all:
            kind = "ExternalOutput"
            out_names.append(name)
        else:
            kind = "Internal"
        env.t[name] = nc.dram_tensor(name, list(shape), bdt, kind=kind)
    for sname in stage_names:
        _, _, fn = STAGES[sname]
        fn(env)
    with nc.Block() as blk:
        @blk.gpsimd
        def _(e):
            for name, v in env.cnt.items():
                if v > 0:
                    e.wait_ge(env.sem[name], v)
    env.close()
    return nc, env, in_names, out_names


def kernel(_export_all=False, **inputs):
    staged = host_prep(inputs)
    live = {}
    for gi, group in enumerate(GROUPS):
        later = set()
        for g2 in GROUPS[gi + 1:]:
            for sname in g2:
                later |= set(STAGES[sname][0])
        nc, env, in_names, out_names = build_group(group, later, export_all=_export_all)
        in_maps = []
        for c in range(NCORES):
            m = {}
            for name in in_names:
                m[name] = staged[c][name] if TENSORS[name][3] == "in" else live[name][c]
            in_maps.append(m)
        res = run_bass_kernel_spmd(nc, in_maps, core_ids=list(range(NCORES)))
        for name in out_names:
            live[name] = [res.results[c][name] for c in range(NCORES)]
    kernel._live = live
    if "y_out" not in live:
        return live
    y = np.stack([live["y_out"][c] for c in range(NCORES)], axis=0)
    y = y.transpose(0, 2, 1).reshape(1, NCORES * TLOC, HID)
    return np.ascontiguousarray(y)
